# version 02
# baseline (speedup 1.0000x reference)
import os
import numpy as np
import concourse.bass as bass
import concourse.mybir as mybir
from concourse.bass_utils import run_bass_kernel_spmd

os.environ.setdefault("TRN_ALLOW_ICI_ALL_TO_ALL", "1")

F32 = mybir.dt.float32
BF16 = mybir.dt.bfloat16
OP = mybir.AluOpType
AF = mybir.ActivationFunctionType
AX = mybir.AxisListType.X

CORES = 8
TOK = 2048
D = 1024
E = 8
NS = 16
NT = TOK // 128
KC = 8
TCN = 4
TCW = 512
WIN = 512
NWIN = (CORES * TOK) // WIN
WPB = NWIN // 2
EPS = 1e-5


def build_nc():
    nc = bass.Bass("TRN2", target_bir_lowering=False, debug=False,
                   detect_race_conditions=False)

    x_slab = nc.dram_tensor("x_slab", [TOK, D], F32, kind="ExternalInput").ap()
    we_t = nc.dram_tensor("we_t", [E, D, D], F32, kind="ExternalInput").ap()
    wg_t = nc.dram_tensor("wg_t", [D, E], F32, kind="ExternalInput").ap()
    be_in = nc.dram_tensor("be_in", [E, D], F32, kind="ExternalInput").ap()
    bg_in = nc.dram_tensor("bg_in", [E, 1], F32, kind="ExternalInput").ap()
    om_in = nc.dram_tensor("om_in", [128, KC], F32, kind="ExternalInput").ap()
    dl_in = nc.dram_tensor("dl_in", [128, NS], F32, kind="ExternalInput").ap()
    al_in = nc.dram_tensor("al_in", [128, NS], F32, kind="ExternalInput").ap()
    bt_in = nc.dram_tensor("bt_in", [128, NS], F32, kind="ExternalInput").ap()
    gm_in = nc.dram_tensor("gm_in", [128, NS], F32, kind="ExternalInput").ap()
    out_slab = nc.dram_tensor("out_slab", [TOK, D], F32, kind="ExternalOutput").ap()

    xn_snd_t = nc.dram_tensor("xn_snd", [CORES, 128, TOK], BF16)
    xn_rcv_t = nc.dram_tensor("xn_rcv", [CORES, 128, TOK], BF16)
    s_snd_t = nc.dram_tensor("s_snd", [CORES, 128, TOK], F32)
    s_rcv_t = nc.dram_tensor("s_rcv", [CORES, 128, TOK], F32)
    g_stage = nc.dram_tensor("g_stage", [D, TOK], BF16)
    w_dram = nc.dram_tensor("w_dram", [E, TOK], BF16)
    xn_snd = xn_snd_t.ap()
    xn_rcv = xn_rcv_t.ap()
    s_snd = s_snd_t.ap()
    s_rcv = s_rcv_t.ap()

    from contextlib import ExitStack
    with ExitStack() as ctx:
        ctx.enter_context(nc.reset_on_exit())
        we_sb = ctx.enter_context(nc.sbuf_tensor("we_sb", [128, E, KC, D], BF16))
        xn_ch = ctx.enter_context(nc.sbuf_tensor("xn_ch", [128, KC, TCW], BF16))
        xn_pack = ctx.enter_context(nc.sbuf_tensor("xn_pack", [128, KC, 128], BF16))
        xn_eb = ctx.enter_context(nc.sbuf_tensor("xn_eb", [128, 2, KC, TCW], BF16))
        wb_sb = ctx.enter_context(nc.sbuf_tensor("wb_sb", [128, 2, E, TCW], BF16))
        g_dt = ctx.enter_context(nc.sbuf_tensor("g_dt", [128, TCW], BF16))
        x_tm = ctx.enter_context(nc.sbuf_tensor("x_tm", [128, D], F32))
        xn_f32 = ctx.enter_context(nc.sbuf_tensor("xn_f32", [128, D], F32))
        st_nmu = ctx.enter_context(nc.sbuf_tensor("st_nmu", [128, NT], F32))
        st_ex2 = ctx.enter_context(nc.sbuf_tensor("st_ex2", [128, NT], F32))
        st_var = ctx.enter_context(nc.sbuf_tensor("st_var", [128, NT], F32))
        st_r = ctx.enter_context(nc.sbuf_tensor("st_r", [128, NT], F32))
        st_sum = ctx.enter_context(nc.sbuf_tensor("st_sum", [128, NT], F32))
        st_aux = ctx.enter_context(nc.sbuf_tensor("st_aux", [128, NT], F32))
        wg_sb = ctx.enter_context(nc.sbuf_tensor("wg_sb", [128, KC, E], BF16))
        be_sb = ctx.enter_context(nc.sbuf_tensor("be_sb", [8, D], BF16))
        bg_sb = ctx.enter_context(nc.sbuf_tensor("bg_sb", [8, 1], F32))
        om_sb = ctx.enter_context(nc.sbuf_tensor("om_sb", [128, KC], F32))
        cf_dl = ctx.enter_context(nc.sbuf_tensor("cf_dl", [128, NS], F32))
        cf_al = ctx.enter_context(nc.sbuf_tensor("cf_al", [128, NS], F32))
        cf_bt = ctx.enter_context(nc.sbuf_tensor("cf_bt", [128, NS], F32))
        cf_gm = ctx.enter_context(nc.sbuf_tensor("cf_gm", [128, NS], F32))
        cf_p = ctx.enter_context(nc.sbuf_tensor("cf_p", [128, NS], F32))
        cf_a = ctx.enter_context(nc.sbuf_tensor("cf_a", [128, NS], F32))
        cf_q = ctx.enter_context(nc.sbuf_tensor("cf_q", [128, NS], F32))
        cf_c = ctx.enter_context(nc.sbuf_tensor("cf_c", [128, NS], F32))
        ident = ctx.enter_context(nc.sbuf_tensor("ident", [128, 128], F32))
        lg_em = ctx.enter_context(nc.sbuf_tensor("lg_em", [8, TCW], F32))
        lg_tm = ctx.enter_context(nc.sbuf_tensor("lg_tm", [128, NT, E], F32))
        g_m1 = ctx.enter_context(nc.sbuf_tensor("g_m1", [128, NT], F32))
        g_m2 = ctx.enter_context(nc.sbuf_tensor("g_m2", [128, NT], F32))
        g_d12 = ctx.enter_context(nc.sbuf_tensor("g_d12", [128, NT], F32))
        g_w1 = ctx.enter_context(nc.sbuf_tensor("g_w1", [128, NT], F32))
        g_w2 = ctx.enter_context(nc.sbuf_tensor("g_w2", [128, NT], F32))
        g_eq1 = ctx.enter_context(nc.sbuf_tensor("g_eq1", [128, 4, E], F32))
        g_eq2 = ctx.enter_context(nc.sbuf_tensor("g_eq2", [128, 4, E], F32))
        g_x1 = ctx.enter_context(nc.sbuf_tensor("g_x1", [128, 4, E], F32))
        w_tm = ctx.enter_context(nc.sbuf_tensor("w_tm", [128, NT, E], F32))
        w_emb = ctx.enter_context(nc.sbuf_tensor("w_emb", [8, TOK], BF16))
        xs_bf = ctx.enter_context(nc.sbuf_tensor("xs_bf", [128, WIN], BF16))
        xs_f = ctx.enter_context(nc.sbuf_tensor("xs_f", [128, WIN], F32))
        h_w = ctx.enter_context(nc.sbuf_tensor("h_w", [128, WIN], F32))
        sacc = ctx.enter_context(nc.sbuf_tensor("sacc", [128, 2, WIN], F32))
        carry = ctx.enter_context(nc.sbuf_tensor("carry", [128, NS], F32))
        eps_c = ctx.enter_context(nc.sbuf_tensor("eps_c", [128, 1], F32))
        s_tile = ctx.enter_context(nc.sbuf_tensor("s_tile", [128, KC, 128], F32))
        gq_tile = ctx.enter_context(nc.sbuf_tensor("gq_tile", [128, KC, 128], BF16))
        t1_sb = ctx.enter_context(nc.sbuf_tensor("t1_sb", [128, 128], F32))
        t2_sb = ctx.enter_context(nc.sbuf_tensor("t2_sb", [128, 2, 128], F32))
        o_stage = ctx.enter_context(nc.sbuf_tensor("o_stage", [128, 2, 128], F32))
        scr = s_tile.ap().rearrange("p a b -> p (a b)")
        ps = ctx.enter_context(nc.psum_tensor("ps", [128, 8, TCW], F32))
        s_sm = ctx.enter_context(nc.semaphore("s_sm"))
        s_cst = ctx.enter_context(nc.semaphore("s_cst"))
        s_x = ctx.enter_context(nc.semaphore("s_x"))
        s_we = [ctx.enter_context(nc.semaphore(f"s_we{i}")) for i in range(E)]
        s_wsm = ctx.enter_context(nc.semaphore("s_wsm"))
        s_id = ctx.enter_context(nc.semaphore("s_id"))
        s_cf1 = ctx.enter_context(nc.semaphore("s_cf1"))
        s_cf2 = ctx.enter_context(nc.semaphore("s_cf2"))
        s_ln = ctx.enter_context(nc.semaphore("s_ln"))
        s_st = ctx.enter_context(nc.semaphore("s_st"))
        s_sq = ctx.enter_context(nc.semaphore("s_sq"))
        s_nrm = ctx.enter_context(nc.semaphore("s_nrm"))
        s_tp = ctx.enter_context(nc.semaphore("s_tp"))
        s_ev = ctx.enter_context(nc.semaphore("s_ev"))
        s_snd1 = ctx.enter_context(nc.semaphore("s_snd1"))
        s_cc1 = ctx.enter_context(nc.semaphore("s_cc1"))
        s_xch = ctx.enter_context(nc.semaphore("s_xch"))
        s_win = ctx.enter_context(nc.semaphore("s_win"))
        s_scn = ctx.enter_context(nc.semaphore("s_scn"))
        s_ssnd = [ctx.enter_context(nc.semaphore(f"s_ssnd{i}")) for i in range(2)]
        s_cc2 = ctx.enter_context(nc.semaphore("s_cc2"))
        s_g1 = ctx.enter_context(nc.semaphore("s_g1"))
        s_g2 = ctx.enter_context(nc.semaphore("s_g2"))
        s_g3 = ctx.enter_context(nc.semaphore("s_g3"))
        s_g4 = ctx.enter_context(nc.semaphore("s_g4"))
        s_sg = ctx.enter_context(nc.semaphore("s_sg"))
        s_g5 = ctx.enter_context(nc.semaphore("s_g5"))
        s_g5b = ctx.enter_context(nc.semaphore("s_g5b"))
        s_g6 = ctx.enter_context(nc.semaphore("s_g6"))
        s_g7 = ctx.enter_context(nc.semaphore("s_g7"))
        s_wdr = ctx.enter_context(nc.semaphore("s_wdr"))
        s_wb = ctx.enter_context(nc.semaphore("s_wb"))
        s_xe = ctx.enter_context(nc.semaphore("s_xe"))
        s_mmE = ctx.enter_context(nc.semaphore("s_mmE"))
        s_mm = ctx.enter_context(nc.semaphore("s_mm"))
        s_si = ctx.enter_context(nc.semaphore("s_si"))
        s_gst = ctx.enter_context(nc.semaphore("s_gst"))
        s_p6l = ctx.enter_context(nc.semaphore("s_p6l"))
        s_p6v = ctx.enter_context(nc.semaphore("s_p6v"))
        s_p6t = ctx.enter_context(nc.semaphore("s_p6t"))
        s_p6e = ctx.enter_context(nc.semaphore("s_p6e"))
        s_out = [ctx.enter_context(nc.semaphore(f"s_out{i}")) for i in range(2)]
        blk = ctx.enter_context(nc.Block())

        @blk.sync
        def _(sy):
            sy.dma_start(out=cf_dl[:], in_=dl_in[:, :]).then_inc(s_sm, 16)
            sy.dma_start(out=cf_al[:], in_=al_in[:, :]).then_inc(s_sm, 16)
            sy.dma_start(out=cf_bt[:], in_=bt_in[:, :]).then_inc(s_sm, 16)
            sy.dma_start(out=cf_gm[:], in_=gm_in[:, :]).then_inc(s_sm, 16)
            sy.dma_start(out=om_sb[:], in_=om_in[:, :]).then_inc(s_sm, 16)
            sy.dma_start(out=bg_sb[:], in_=bg_in[:, :]).then_inc(s_sm, 16)
            for i in range(NT):
                if i >= 1:
                    sy.wait_ge(s_ln, i)
                sy.dma_start(out=x_tm[:, :],
                             in_=x_slab[i * 128:(i + 1) * 128, :]).then_inc(s_x, 16)
            for i in range(NT):
                if i == 0:
                    sy.wait_ge(s_ln, NT)
                else:
                    sy.wait_ge(s_nrm, i)
                sy.dma_start(out=x_tm[:, :],
                             in_=x_slab[i * 128:(i + 1) * 128, :]).then_inc(s_x, 16)
                if i >= 1:
                    ip = i - 1
                    sy.wait_ge(s_ev, 8 * (ip + 1))
                    sy.dma_start(out=xn_snd.rearrange("kc p t -> p kc t")
                                 [:, :, ip * 128:(ip + 1) * 128],
                                 in_=xn_pack[:, :, :]).then_inc(s_snd1, 16)
            ip = NT - 1
            sy.wait_ge(s_ev, 8 * (ip + 1))
            sy.dma_start(out=xn_snd.rearrange("kc p t -> p kc t")
                         [:, :, ip * 128:(ip + 1) * 128],
                         in_=xn_pack[:, :, :]).then_inc(s_snd1, 16)
            for tc in range(TCN):
                if tc >= 1:
                    sy.wait_ge(s_g1, tc)
                sy.dma_start(out=xn_ch[:, :, :],
                             in_=xn_snd.rearrange("kc p t -> p kc t")
                             [:, :, tc * TCW:(tc + 1) * TCW]).then_inc(s_xch, 16)
            sy.wait_ge(s_g7, NT)
            sy.dma_start(out=w_dram.ap(), in_=w_emb[:, :]).then_inc(s_wdr, 16)
            for tc in range(TCN):
                if tc == 0:
                    sy.wait_ge(s_g1, TCN)
                else:
                    sy.wait_ge(s_xe, 8 * tc)
                sy.dma_start(out=xn_ch[:, :, :],
                             in_=xn_snd.rearrange("kc p t -> p kc t")
                             [:, :, tc * TCW:(tc + 1) * TCW]).then_inc(s_xch, 16)
            for tt in range(NT):
                if tt == 0:
                    sy.wait_ge(s_xe, 8 * TCN)
                else:
                    sy.wait_ge(s_p6v, 16 * tt)
                sy.dma_start(out=xn_ch[:, :, 0:128],
                             in_=xn_snd.rearrange("kc p t -> p kc t")
                             [:, :, tt * 128:(tt + 1) * 128]).then_inc(s_xch, 16)
                sy.wait_ge(s_p6e, 8 * tt)
                sy.dma_start(out=x_tm[:, :],
                             in_=x_slab[tt * 128:(tt + 1) * 128, :]
                             ).then_inc(s_x, 16)
                for i in range(KC):
                    idx = tt * KC + i
                    sy.wait_ge(s_p6e, idx + 1)
                    sy.dma_start(out=out_slab[tt * 128:(tt + 1) * 128,
                                              i * 128:(i + 1) * 128],
                                 in_=o_stage[:, idx % 2, :]
                                 ).then_inc(s_out[idx % 2], 16)
            sy.wait_ge(s_out[0], 16 * NT * KC // 2)
            sy.wait_ge(s_out[1], 16 * NT * KC // 2)

        @blk.gpsimd
        def _(g):
            for e in range(E):
                g.dma_start(out=we_sb[:, e, :, :],
                            in_=we_t[e].rearrange("(kc p) h -> p kc h", p=128)
                            ).then_inc(s_we[e], 16)
            g.memset(ident[:, :], 0.0)
            g.affine_select(out=ident[:, :], in_=ident[:, :],
                            compare_op=OP.not_equal, fill=1.0, base=0,
                            pattern=[[-1, 128]], channel_multiplier=1
                            ).then_inc(s_id, 1)
            g.dma_start(out=be_sb[:, :], in_=be_in[:, :]).then_inc(s_wsm, 16)
            g.dma_start(out=wg_sb[:],
                        in_=wg_t.rearrange("(kc p) e -> p kc e", p=128)
                        ).then_inc(s_wsm, 16)
            g.wait_ge(s_snd1, 16 * NT)
            g.collective_compute(
                "AllToAll", OP.bypass,
                replica_groups=[list(range(CORES))],
                ins=[xn_snd_t.ap().opt()], outs=[xn_rcv_t.ap().opt()],
            ).then_inc(s_cc1, 1)
            g.wait_ge(s_cc1, 1)
            for w in range(NWIN):
                if w >= 1:
                    g.wait_ge(s_cst, w)
                g.dma_start(out=xs_bf[:, :],
                            in_=xn_rcv[w // 4, :, (w % 4) * WIN:(w % 4 + 1) * WIN]
                            ).then_inc(s_win, 16)
                if w >= 1:
                    wp = w - 1
                    g.wait_ge(s_scn, wp + 1)
                    g.dma_start(out=s_snd[wp // 4, :, (wp % 4) * WIN:(wp % 4 + 1) * WIN],
                                in_=sacc[:, wp % 2, :]).then_inc(s_ssnd[wp % 2], 16)
            wp = NWIN - 1
            g.wait_ge(s_scn, wp + 1)
            g.dma_start(out=s_snd[wp // 4, :, (wp % 4) * WIN:(wp % 4 + 1) * WIN],
                        in_=sacc[:, wp % 2, :]).then_inc(s_ssnd[wp % 2], 16)
            g.wait_ge(s_ssnd[0], 16 * (NWIN // 2))
            g.wait_ge(s_ssnd[1], 16 * (NWIN // 2))
            g.collective_compute(
                "AllToAll", OP.bypass,
                replica_groups=[list(range(CORES))],
                ins=[s_snd_t.ap().opt()], outs=[s_rcv_t.ap().opt()],
            ).then_inc(s_cc2, 1)
            g.wait_ge(s_cc2, 1)
            for tt in range(NT):
                if tt >= 1:
                    g.wait_ge(s_p6v, 16 * tt)
                g.dma_start(out=s_tile[:, :, :],
                            in_=s_rcv.rearrange("i ch t -> ch i t")
                            [:, :, tt * 128:(tt + 1) * 128]).then_inc(s_p6l, 16)
                g.wait_ge(s_gst, 16 * 8 * (tt // 4 + 1))
                g.dma_start(out=gq_tile[:, :, :],
                            in_=g_stage.ap().rearrange("(dd p) t -> p dd t", p=128)
                            [:, :, tt * 128:(tt + 1) * 128]).then_inc(s_p6l, 16)

        @blk.scalar
        def _(sc):
            sc.wait_ge(s_sm, 16 * 6)
            sc.activation(out=cf_p[:], in_=cf_dl[:], func=AF.Sigmoid)
            sc.activation(out=cf_a[:], in_=cf_al[:], func=AF.Sigmoid).then_inc(s_cf1, 1)
            sc.wait_ge(s_st, 1)
            sc.activation(out=st_sum[:, :], in_=st_var[:, :],
                          func=AF.Sqrt, bias=eps_c[:, 0:1]).then_inc(s_sq, 1)
            for q in range(4):
                sc.wait_ge(s_g5, q + 1)
                sc.activation(out=g_w1[:, 4 * q:4 * q + 4],
                              in_=g_d12[:, 4 * q:4 * q + 4],
                              func=AF.Sigmoid).then_inc(s_sg, 1)
            sc.activation(out=g_dt[:, 0:128], in_=ident[:, 0:128],
                          func=AF.Silu)
            sc.wait_ge(s_wdr, 16)
            for tc in range(min(2, TCN)):
                for e in range(E):
                    sc.dma_start(out=wb_sb[:, tc % 2, e, :],
                                 in_=w_dram[e:e + 1, tc * TCW:(tc + 1) * TCW]
                                 .broadcast_to([128, TCW])).then_inc(s_wb, 16)
            for tc in range(TCN):
                for dout in range(8):
                    k = tc * 8 + dout
                    sc.wait_ge(s_mm, 8 * (tc + 1))
                    if k >= 1:
                        sc.wait_ge(s_gst, 16 * k)
                    sc.activation(out=g_dt[:, :], in_=ps[:, dout, :],
                                  func=AF.Silu).then_inc(s_si, 1)
                    sc.dma_start(out=g_stage.ap()
                                 .rearrange("(dd p) t -> p dd t", p=128)
                                 [:, dout, tc * TCW:(tc + 1) * TCW],
                                 in_=g_dt[:, :]).then_inc(s_gst, 16)
                if tc + 2 < TCN:
                    sc.wait_ge(s_xe, 8 * (tc + 1))
                    for e in range(E):
                        sc.dma_start(out=wb_sb[:, tc % 2, e, :],
                                     in_=w_dram[e:e + 1,
                                                (tc + 2) * TCW:(tc + 3) * TCW]
                                     .broadcast_to([128, TCW])).then_inc(s_wb, 16)

        @blk.tensor
        def _(t):
            t.wait_ge(s_id, 1)
            for i in range(NT):
                t.wait_ge(s_nrm, i + 1)
                for kc in range(KC):
                    idx = i * KC + kc
                    if idx >= 2:
                        t.wait_ge(s_ev, idx - 1)
                    t.transpose(ps[:, 1 + idx % 2, 0:128],
                                xn_f32[:, kc * 128:(kc + 1) * 128],
                                ident[:, :]).then_inc(s_tp, 1)
            t.wait_ge(s_wsm, 32)
            for q in range(TCN):
                t.wait_ge(s_xch, 16 * (q + 1))
                if q >= 1:
                    t.wait_ge(s_g2, q)
                for kc in range(KC):
                    t.matmul(ps[0:8, 0, :], wg_sb[:, kc, :],
                             xn_ch[:, kc, :],
                             start=(kc == 0), stop=(kc == KC - 1)
                             ).then_inc(s_g1, 1 if kc == KC - 1 else 0)
                t.wait_ge(s_g2, q + 1)
                for j in range(4):
                    tt_i = 4 * q + j
                    if tt_i >= 2:
                        t.wait_ge(s_g4, tt_i - 1)
                    t.transpose(ps[0:128, 1 + tt_i % 2, 0:8],
                                lg_em[0:8, j * 128:(j + 1) * 128],
                                ident[0:8, 0:8]).then_inc(s_g3, 1)
            for q in range(4):
                t.wait_ge(s_g5b, q + 1)
                for j in range(4):
                    tt_i = 4 * q + j
                    if tt_i >= 2:
                        t.wait_ge(s_g7, tt_i - 1)
                    t.transpose(ps[0:8, 4 + tt_i % 2, 0:128],
                                w_tm[:, tt_i, :],
                                ident[:, :]).then_inc(s_g6, 1)
            t.wait_ge(s_g7, NT)
            for tc in range(TCN):
                if tc >= 1:
                    t.wait_ge(s_si, 8 * tc)
                for e in range(E):
                    t.wait_ge(s_xe, tc * 8 + e + 1)
                    t.wait_ge(s_we[e], 16)
                    ins = None
                    for dout in range(8):
                        for kc in range(KC):
                            ins = t.matmul(
                                ps[:, dout, :],
                                we_sb[:, e, kc, dout * 128:(dout + 1) * 128],
                                xn_eb[:, e % 2, kc, :],
                                start=(e == 0 and kc == 0), stop=False)
                    ins.then_inc(s_mmE, 1)
                for dout in range(8):
                    t.matmul(ps[:, dout, :], be_sb[0:8, dout * 128:(dout + 1) * 128],
                             w_emb[0:8, tc * TCW:(tc + 1) * TCW],
                             start=False, stop=True).then_inc(s_mm, 1)
            t.wait_ge(s_si, 8 * TCN)
            for idx in range(NT * KC):
                t.wait_ge(s_p6v, 2 * (idx + 1))
                if idx >= 2:
                    t.wait_ge(s_p6e, idx - 1)
                t.transpose(ps[:, 1 + idx % 2, 0:128],
                            t2_sb[:, idx % 2, :], ident[:, :]).then_inc(s_p6t, 1)

        @blk.vector
        def _(v):
            v.memset(eps_c[:, :], EPS)

            def spacer():
                v.tensor_copy(out=xs_f[:, 0:128], in_=ident[:, 0:128])

            for i in range(NT):
                v.wait_ge(s_x, 16 * (i + 1))
                v.reduce_sum(st_sum[:, i:i + 1], x_tm[:, :], axis=AX)
                v.tensor_tensor(out=scr[:], in0=x_tm[:, :], in1=x_tm[:, :],
                                op=OP.mult)
                v.reduce_sum(st_ex2[:, i:i + 1], scr[:], axis=AX
                             ).then_inc(s_ln, 1)
            spacer()
            v.tensor_scalar(out=st_nmu[:, :], in0=st_sum[:, :],
                            scalar1=-1.0 / D, scalar2=None, op0=OP.mult)
            spacer()
            v.tensor_scalar(out=st_ex2[:, :], in0=st_ex2[:, :],
                            scalar1=1.0 / D, scalar2=None, op0=OP.mult)
            spacer()
            v.tensor_tensor(out=st_var[:, :], in0=st_nmu[:, :], in1=st_nmu[:, :],
                            op=OP.mult)
            spacer()
            v.tensor_tensor(out=st_var[:, :], in0=st_ex2[:, :], in1=st_var[:, :],
                            op=OP.subtract)
            v.tensor_copy(out=xs_f[:, 0:128], in_=ident[:, 0:128]
                          ).then_inc(s_st, 1)
            v.wait_ge(s_sq, 1)
            spacer()
            v.reciprocal(out=st_r[:, :], in_=st_sum[:, :])
            v.tensor_scalar(out=st_ex2[:, :], in0=st_var[:, :],
                            scalar1=EPS, scalar2=None, op0=OP.add)
            spacer()
            v.tensor_tensor(out=st_aux[:, :], in0=st_r[:, :], in1=st_r[:, :],
                            op=OP.mult)
            spacer()
            v.tensor_tensor(out=st_aux[:, :], in0=st_aux[:, :], in1=st_ex2[:, :],
                            op=OP.mult)
            spacer()
            v.tensor_scalar(out=st_aux[:, :], in0=st_aux[:, :],
                            scalar1=-0.5, scalar2=1.5, op0=OP.mult, op1=OP.add)
            spacer()
            v.tensor_tensor(out=st_r[:, :], in0=st_r[:, :], in1=st_aux[:, :],
                            op=OP.mult)
            spacer()
            v.wait_ge(s_cf1, 1)
            v.wait_ge(s_sm, 16 * 6)
            spacer()
            v.tensor_tensor(out=cf_q[:], in0=cf_p[:], in1=cf_a[:], op=OP.mult)
            v.tensor_tensor(out=st_sum[:, :], in0=cf_bt[:], in1=cf_gm[:],
                            op=OP.mult)
            v.tensor_scalar(out=st_aux[:, :], in0=cf_p[:],
                            scalar1=0.25, scalar2=None, op0=OP.mult)
            v.tensor_scalar(out=cf_q[:], in0=cf_q[:], scalar1=-1.0, scalar2=1.0,
                            op0=OP.mult, op1=OP.add)
            v.tensor_tensor(out=cf_c[:], in0=st_sum[:, :], in1=st_aux[:, :],
                            op=OP.mult)
            spacer()
            v.tensor_copy(out=st_aux[:, 0:1], in_=cf_c[:, 0:1]).then_inc(s_cf2, 1)
            for i in range(NT):
                v.wait_ge(s_x, 16 * (NT + i + 1))
                if i >= 1:
                    v.wait_ge(s_tp, 8 * i)
                v.tensor_scalar(out=xn_f32[:, :], in0=x_tm[:, :],
                                scalar1=st_nmu[:, i:i + 1],
                                scalar2=st_r[:, i:i + 1],
                                op0=OP.add, op1=OP.mult)
                v.tensor_copy(out=xs_f[:, 0:128], in_=ident[:, 0:128]
                              ).then_inc(s_nrm, 1)
                for kc in range(KC):
                    idx = i * KC + kc
                    if kc == 0 and i >= 1:
                        v.wait_ge(s_snd1, 16 * i)
                    v.wait_ge(s_tp, idx + 1)
                    v.tensor_copy(out=xn_pack[:, kc, :],
                                  in_=ps[:, 1 + idx % 2, 0:128]
                                  ).then_inc(s_ev, 1)
            for q in range(TCN):
                v.wait_ge(s_g1, q + 1)
                if q >= 1:
                    v.wait_ge(s_g3, 4 * q)
                v.tensor_scalar(out=lg_em[0:8, :], in0=ps[0:8, 0, :],
                                scalar1=bg_sb[0:8, 0:1], scalar2=None,
                                op0=OP.add)
                v.tensor_copy(out=xs_f[:, 0:128], in_=ident[:, 0:128]
                              ).then_inc(s_g2, 1)
                for j in range(4):
                    tt_i = 4 * q + j
                    v.wait_ge(s_g3, tt_i + 1)
                    v.tensor_copy(out=lg_tm[:, tt_i, :],
                                  in_=ps[0:128, 1 + tt_i % 2, 0:8]
                                  ).then_inc(s_g4, 1)
                tls = list(range(4 * q, 4 * q + 4))
                for ti in tls:
                    v.reduce_max(g_m1[:, ti:ti + 1], lg_tm[:, ti, :], axis=AX)
                for ti in tls:
                    v.tensor_scalar(out=g_eq1[:, ti % 4, :], in0=lg_tm[:, ti, :],
                                    scalar1=g_m1[:, ti:ti + 1], scalar2=None,
                                    op0=OP.is_equal)
                for ti in tls:
                    v.scalar_tensor_tensor(out=g_x1[:, ti % 4, :],
                                           in0=g_eq1[:, ti % 4, :],
                                           scalar=-1e30, in1=lg_tm[:, ti, :],
                                           op0=OP.mult, op1=OP.add)
                for ti in tls:
                    v.reduce_max(g_m2[:, ti:ti + 1], g_x1[:, ti % 4, :], axis=AX)
                for ti in tls:
                    v.tensor_scalar(out=g_eq2[:, ti % 4, :], in0=g_x1[:, ti % 4, :],
                                    scalar1=g_m2[:, ti:ti + 1], scalar2=None,
                                    op0=OP.is_equal)
                v.tensor_tensor(out=g_d12[:, 4 * q:4 * q + 4],
                                in0=g_m1[:, 4 * q:4 * q + 4],
                                in1=g_m2[:, 4 * q:4 * q + 4],
                                op=OP.subtract).then_inc(s_g5, 1)
                v.wait_ge(s_sg, q + 1)
                spacer()
                v.tensor_scalar(out=g_w2[:, 4 * q:4 * q + 4],
                                in0=g_w1[:, 4 * q:4 * q + 4],
                                scalar1=-1.0, scalar2=1.0, op0=OP.mult, op1=OP.add)
                for ti in tls:
                    v.tensor_scalar(out=g_x1[:, ti % 4, :], in0=g_eq1[:, ti % 4, :],
                                    scalar1=g_w1[:, ti:ti + 1], scalar2=None,
                                    op0=OP.mult)
                for ti in tls:
                    v.scalar_tensor_tensor(out=w_tm[:, ti, :],
                                           in0=g_eq2[:, ti % 4, :],
                                           scalar=g_w2[:, ti:ti + 1],
                                           in1=g_x1[:, ti % 4, :],
                                           op0=OP.mult, op1=OP.add)
                spacer()
                v.tensor_copy(out=xs_f[:, 0:128], in_=ident[:, 0:128]
                              ).then_inc(s_g5b, 1)
            for tt_i in range(NT):
                v.wait_ge(s_g6, tt_i + 1)
                v.tensor_copy(out=w_emb[0:8, tt_i * 128:(tt_i + 1) * 128],
                              in_=ps[0:8, 4 + tt_i % 2, 0:128]
                              ).then_inc(s_g7, 1)
            scan_work = []

            def emit_scans(budget):
                n = 0
                while scan_work and n < budget:
                    scan_work.pop(0)()
                    n += 1

            for w in range(NWIN):
                def cast(w=w):
                    v.wait_ge(s_win, 16 * (w + 1))
                    if w >= 2:
                        v.wait_ge(s_ssnd[w % 2], 16 * (w // 2))
                    if w == 0:
                        v.wait_ge(s_cf2, 1)
                    v.tensor_copy(out=xs_f[:, :], in_=xs_bf[:, :]
                                  ).then_inc(s_cst, 1)
                scan_work.append(cast)
                for n in range(NS):
                    def scan(w=w, n=n):
                        q_b = cf_q[:, n:n + 1].broadcast_to([128, WIN])
                        if w % WPB == 0:
                            v.tensor_tensor_scan(out=h_w[:, :], data0=q_b,
                                                 data1=xs_f[:, :], initial=0.0,
                                                 op0=OP.mult, op1=OP.add)
                        else:
                            v.tensor_tensor_scan(out=h_w[:, :], data0=q_b,
                                                 data1=xs_f[:, :],
                                                 initial=carry[:, n:n + 1],
                                                 op0=OP.mult, op1=OP.add)

                    def comb(w=w, n=n):
                        if n == 0:
                            v.tensor_scalar(out=sacc[:, w % 2, :], in0=h_w[:, :],
                                            scalar1=cf_c[:, 0:1], scalar2=None,
                                            op0=OP.mult)
                        else:
                            v.scalar_tensor_tensor(out=sacc[:, w % 2, :],
                                                   in0=h_w[:, :],
                                                   scalar=cf_c[:, n:n + 1],
                                                   in1=sacc[:, w % 2, :],
                                                   op0=OP.mult, op1=OP.add)

                    def carrycp(w=w, n=n):
                        ins = v.tensor_copy(out=carry[:, n:n + 1],
                                            in_=h_w[:, WIN - 1:WIN])
                        if n == NS - 1:
                            ins.then_inc(s_scn, 1)
                    scan_work.append(scan)
                    scan_work.append(comb)
                    scan_work.append(carrycp)

            per_slot = (len(scan_work) + TCN * E - 1) // (TCN * E)
            for tc in range(TCN):
                for e in range(E):
                    gi = tc * 8 + e
                    if gi >= 2:
                        v.wait_ge(s_mmE, gi - 1)
                    v.wait_ge(s_wb, 256 if tc < 2 else 256 + 128 * (tc - 1))
                    v.wait_ge(s_xch, 16 * (TCN + tc + 1))
                    v.tensor_tensor(
                        out=xn_eb[:, e % 2, :, :],
                        in0=xn_ch[:, :, :],
                        in1=wb_sb[:, tc % 2, e, :]
                        .rearrange("p (a t) -> p a t", a=1)
                        .broadcast_to([128, KC, TCW]),
                        op=OP.mult)
                    v.tensor_copy(out=t1_sb[:, :], in_=ident[:, :])
                    v.tensor_copy(out=t1_sb[:, :], in_=ident[:, :]
                                  ).then_inc(s_xe, 1)
                    emit_scans(per_slot)
            emit_scans(10 ** 9)
            def p6_evac(ix, also_p6v):
                v.wait_ge(s_p6t, ix + 1)
                if ix >= 2:
                    v.wait_ge(s_out[ix % 2], 16 * (ix // 2))
                v.tensor_tensor(out=o_stage[:, ix % 2, :],
                                in0=ps[:, 1 + ix % 2, 0:128],
                                in1=x_tm[:, (ix % KC) * 128:(ix % KC) * 128 + 128],
                                op=OP.add).then_inc(s_p6e, 1)
                if also_p6v:
                    v.tensor_copy(out=xs_f[:, 0:8], in_=ident[:, 0:8]
                                  ).then_inc(s_p6v, 2)

            for idx in range(NT * KC):
                tt, i = divmod(idx, KC)
                if i == 0 and idx >= 1:
                    p6_evac(idx - 1, False)
                if i == 0:
                    v.wait_ge(s_p6l, 32 * (tt + 1))
                    v.wait_ge(s_xch, 16 * (2 * TCN + tt + 1))
                    v.wait_ge(s_x, 16 * (2 * NT + tt + 1))
                v.scalar_tensor_tensor(out=t1_sb[:, :],
                                       in0=xn_ch[:, i, 0:128],
                                       scalar=om_sb[:, i:i + 1],
                                       in1=s_tile[:, i, :],
                                       op0=OP.mult, op1=OP.add)
                v.tensor_tensor(out=t2_sb[:, idx % 2, :], in0=t1_sb[:, :],
                                in1=gq_tile[:, i, :], op=OP.mult)
                if i == 0:
                    v.tensor_copy(out=xs_f[:, 0:128], in_=ident[:, 0:128]
                                  ).then_inc(s_p6v, 2)
                else:
                    p6_evac(idx - 1, True)
            p6_evac(NT * KC - 1, False)
    return nc


_NC_CACHE = {}


def _get_nc():
    if "nc" not in _NC_CACHE:
        _NC_CACHE["nc"] = build_nc()
    return _NC_CACHE["nc"]


def kernel(L0_attn_x, L0_attn_delta, L0_attn_alpha, L0_attn_beta, L0_attn_gamma,
           L0_attn_omega, L0_ffn_We, L0_ffn_be, L0_ffn_Wg, L0_ffn_bg):
    x = np.ascontiguousarray(np.asarray(L0_attn_x, dtype=np.float32)
                             .reshape(16384, 1024))
    we_t = np.ascontiguousarray(np.asarray(L0_ffn_We, np.float32).transpose(0, 2, 1))
    wg_t = np.ascontiguousarray(np.asarray(L0_ffn_Wg, np.float32).T)
    be = np.ascontiguousarray(np.asarray(L0_ffn_be, np.float32))
    bg = np.ascontiguousarray(np.asarray(L0_ffn_bg, np.float32).reshape(8, 1))
    om = np.ascontiguousarray(np.asarray(L0_attn_omega, np.float32)
                              .reshape(8, 128).T)
    dl = np.asarray(L0_attn_delta, np.float32)[:, :, 0]
    al = np.asarray(L0_attn_alpha, np.float32)[:, :, 0]
    bt = np.asarray(L0_attn_beta, np.float32)[:, :, 0]
    gm = np.asarray(L0_attn_gamma, np.float32)

    in_maps = []
    for j in range(CORES):
        sl = slice(j * TOK, (j + 1) * TOK)
        ch = slice(j * 128, (j + 1) * 128)
        in_maps.append({
            "x_slab": np.ascontiguousarray(x[sl]),
            "we_t": we_t, "wg_t": wg_t, "be_in": be, "bg_in": bg, "om_in": om,
            "dl_in": np.ascontiguousarray(dl[ch]),
            "al_in": np.ascontiguousarray(al[ch]),
            "bt_in": np.ascontiguousarray(bt[ch]),
            "gm_in": np.ascontiguousarray(gm[ch]),
        })
    nc = _get_nc()
    res = run_bass_kernel_spmd(nc, in_maps, core_ids=list(range(CORES)))
    out = np.concatenate([r["out_slab"] for r in res.results], axis=0)
    return out.reshape(2, 8192, 1024)


# version 03
# speedup vs baseline: 1.0572x; 1.0572x over previous
import os
import numpy as np
import concourse.bass as bass
import concourse.mybir as mybir
from concourse.bass_utils import run_bass_kernel_spmd

os.environ.setdefault("TRN_ALLOW_ICI_ALL_TO_ALL", "1")

F32 = mybir.dt.float32
BF16 = mybir.dt.bfloat16
OP = mybir.AluOpType
AF = mybir.ActivationFunctionType
AX = mybir.AxisListType.X

CORES = 8
TOK = 2048
D = 1024
E = 8
NS = 16
NT = TOK // 128
KC = 8
TCN = 4
TCW = 512
WIN = 512
NWIN = (CORES * TOK) // WIN
WPB = NWIN // 2
EPS = 1e-5


def build_nc():
    nc = bass.Bass("TRN2", target_bir_lowering=False, debug=False,
                   detect_race_conditions=False)

    x_slab = nc.dram_tensor("x_slab", [TOK, D], F32, kind="ExternalInput").ap()
    we_t = nc.dram_tensor("we_t", [E, D, D], F32, kind="ExternalInput").ap()
    wg_t = nc.dram_tensor("wg_t", [D, E], F32, kind="ExternalInput").ap()
    be_in = nc.dram_tensor("be_in", [E, D], F32, kind="ExternalInput").ap()
    bg_in = nc.dram_tensor("bg_in", [E, 1], F32, kind="ExternalInput").ap()
    om_in = nc.dram_tensor("om_in", [128, KC], F32, kind="ExternalInput").ap()
    dl_in = nc.dram_tensor("dl_in", [128, NS], F32, kind="ExternalInput").ap()
    al_in = nc.dram_tensor("al_in", [128, NS], F32, kind="ExternalInput").ap()
    bt_in = nc.dram_tensor("bt_in", [128, NS], F32, kind="ExternalInput").ap()
    gm_in = nc.dram_tensor("gm_in", [128, NS], F32, kind="ExternalInput").ap()
    out_slab = nc.dram_tensor("out_slab", [TOK, D], F32, kind="ExternalOutput").ap()

    xn_snd_t = nc.dram_tensor("xn_snd", [CORES, 128, TOK], BF16)
    xn_rcv_t = nc.dram_tensor("xn_rcv", [CORES, 128, TOK], BF16)
    s_snd_t = nc.dram_tensor("s_snd", [CORES, 128, TOK], BF16)
    s_rcv_t = nc.dram_tensor("s_rcv", [CORES, 128, TOK], BF16)
    g_stage = nc.dram_tensor("g_stage", [D, TOK], BF16)
    w_dram = nc.dram_tensor("w_dram", [E, TOK], BF16)
    xn_snd = xn_snd_t.ap()
    xn_rcv = xn_rcv_t.ap()
    s_snd = s_snd_t.ap()
    s_rcv = s_rcv_t.ap()

    from contextlib import ExitStack
    with ExitStack() as ctx:
        ctx.enter_context(nc.reset_on_exit())
        we_sb = ctx.enter_context(nc.sbuf_tensor("we_sb", [128, E, KC, D], BF16))
        xn_ch = ctx.enter_context(nc.sbuf_tensor("xn_ch", [128, KC, TCW], BF16))
        xn_pack = ctx.enter_context(nc.sbuf_tensor("xn_pack", [128, KC, 128], BF16))
        xn_eb = ctx.enter_context(nc.sbuf_tensor("xn_eb", [128, KC, TCW], BF16))
        wb_sb = ctx.enter_context(nc.sbuf_tensor("wb_sb", [128, 2, E, TCW], BF16))
        g_dt = ctx.enter_context(nc.sbuf_tensor("g_dt", [128, TCW], BF16))
        x_tm = ctx.enter_context(nc.sbuf_tensor("x_tm", [128, 2, D], F32))
        xn_f32 = ctx.enter_context(nc.sbuf_tensor("xn_f32", [128, D], F32))
        st_nmu = ctx.enter_context(nc.sbuf_tensor("st_nmu", [128, NT], F32))
        st_ex2 = ctx.enter_context(nc.sbuf_tensor("st_ex2", [128, NT], F32))
        st_var = ctx.enter_context(nc.sbuf_tensor("st_var", [128, NT], F32))
        st_r = ctx.enter_context(nc.sbuf_tensor("st_r", [128, NT], F32))
        st_sum = ctx.enter_context(nc.sbuf_tensor("st_sum", [128, NT], F32))
        st_aux = ctx.enter_context(nc.sbuf_tensor("st_aux", [128, NT], F32))
        wg_sb = ctx.enter_context(nc.sbuf_tensor("wg_sb", [128, KC, E], BF16))
        be_sb = ctx.enter_context(nc.sbuf_tensor("be_sb", [8, D], BF16))
        bg_sb = ctx.enter_context(nc.sbuf_tensor("bg_sb", [8, 1], F32))
        om_sb = ctx.enter_context(nc.sbuf_tensor("om_sb", [128, KC], F32))
        cf_dl = ctx.enter_context(nc.sbuf_tensor("cf_dl", [128, NS], F32))
        cf_al = ctx.enter_context(nc.sbuf_tensor("cf_al", [128, NS], F32))
        cf_bt = ctx.enter_context(nc.sbuf_tensor("cf_bt", [128, NS], F32))
        cf_gm = ctx.enter_context(nc.sbuf_tensor("cf_gm", [128, NS], F32))
        cf_p = ctx.enter_context(nc.sbuf_tensor("cf_p", [128, NS], F32))
        cf_a = ctx.enter_context(nc.sbuf_tensor("cf_a", [128, NS], F32))
        cf_q = ctx.enter_context(nc.sbuf_tensor("cf_q", [128, NS], F32))
        cf_c = ctx.enter_context(nc.sbuf_tensor("cf_c", [128, NS], F32))
        ident = ctx.enter_context(nc.sbuf_tensor("ident", [128, 128], F32))
        lg_em = ctx.enter_context(nc.sbuf_tensor("lg_em", [8, TCW], F32))
        lg_tm = ctx.enter_context(nc.sbuf_tensor("lg_tm", [128, NT, E], F32))
        g_m1 = ctx.enter_context(nc.sbuf_tensor("g_m1", [128, NT], F32))
        g_m2 = ctx.enter_context(nc.sbuf_tensor("g_m2", [128, NT], F32))
        g_d12 = ctx.enter_context(nc.sbuf_tensor("g_d12", [128, NT], F32))
        g_w1 = ctx.enter_context(nc.sbuf_tensor("g_w1", [128, NT], F32))
        g_w2 = ctx.enter_context(nc.sbuf_tensor("g_w2", [128, NT], F32))
        g_eq1 = ctx.enter_context(nc.sbuf_tensor("g_eq1", [128, 4, E], F32))
        g_eq2 = ctx.enter_context(nc.sbuf_tensor("g_eq2", [128, 4, E], F32))
        g_x1 = ctx.enter_context(nc.sbuf_tensor("g_x1", [128, 4, E], F32))
        w_tm = ctx.enter_context(nc.sbuf_tensor("w_tm", [128, NT, E], F32))
        w_emb = ctx.enter_context(nc.sbuf_tensor("w_emb", [8, TOK], BF16))
        xs_bf = ctx.enter_context(nc.sbuf_tensor("xs_bf", [128, WIN], BF16))
        xs_f = ctx.enter_context(nc.sbuf_tensor("xs_f", [128, WIN], F32))
        h_w = ctx.enter_context(nc.sbuf_tensor("h_w", [128, WIN], F32))
        sacc = ctx.enter_context(nc.sbuf_tensor("sacc", [128, 2, WIN], F32))
        carry = ctx.enter_context(nc.sbuf_tensor("carry", [128, NS], F32))
        scr = sacc.ap().rearrange("p a b -> p (a b)")
        eps_c = ctx.enter_context(nc.sbuf_tensor("eps_c", [128, 1], F32))
        s_tile = ctx.enter_context(nc.sbuf_tensor("s_tile", [128, 2, KC, 128], BF16))
        gq_tile = ctx.enter_context(nc.sbuf_tensor("gq_tile", [128, KC, 128], BF16))
        t1_sb = ctx.enter_context(nc.sbuf_tensor("t1_sb", [128, 128], F32))
        t2_sb = ctx.enter_context(nc.sbuf_tensor("t2_sb", [128, 2, 128], F32))
        o_stage = ctx.enter_context(nc.sbuf_tensor("o_stage", [128, 3, 128], F32))
        ps = ctx.enter_context(nc.psum_tensor("ps", [128, 8, TCW], F32))
        s_sm = ctx.enter_context(nc.semaphore("s_sm"))
        s_cst = ctx.enter_context(nc.semaphore("s_cst"))
        s_x = [ctx.enter_context(nc.semaphore(f"s_x{i}")) for i in range(2)]
        s_we = [ctx.enter_context(nc.semaphore(f"s_we{i}")) for i in range(E)]
        s_wsm = ctx.enter_context(nc.semaphore("s_wsm"))
        s_id = ctx.enter_context(nc.semaphore("s_id"))
        s_cf1 = ctx.enter_context(nc.semaphore("s_cf1"))
        s_cf2 = ctx.enter_context(nc.semaphore("s_cf2"))
        s_ln = ctx.enter_context(nc.semaphore("s_ln"))
        s_st = ctx.enter_context(nc.semaphore("s_st"))
        s_sq = ctx.enter_context(nc.semaphore("s_sq"))
        s_nrm = ctx.enter_context(nc.semaphore("s_nrm"))
        s_tp = ctx.enter_context(nc.semaphore("s_tp"))
        s_ev = ctx.enter_context(nc.semaphore("s_ev"))
        s_snd1 = ctx.enter_context(nc.semaphore("s_snd1"))
        s_cc1 = ctx.enter_context(nc.semaphore("s_cc1"))
        s_xch = ctx.enter_context(nc.semaphore("s_xch"))
        s_win = ctx.enter_context(nc.semaphore("s_win"))
        s_scn = ctx.enter_context(nc.semaphore("s_scn"))
        s_ssnd = [ctx.enter_context(nc.semaphore(f"s_ssnd{i}")) for i in range(2)]
        s_cc2 = ctx.enter_context(nc.semaphore("s_cc2"))
        s_g1 = ctx.enter_context(nc.semaphore("s_g1"))
        s_g2 = ctx.enter_context(nc.semaphore("s_g2"))
        s_g3 = ctx.enter_context(nc.semaphore("s_g3"))
        s_g4 = ctx.enter_context(nc.semaphore("s_g4"))
        s_sg = ctx.enter_context(nc.semaphore("s_sg"))
        s_g5 = ctx.enter_context(nc.semaphore("s_g5"))
        s_g5b = ctx.enter_context(nc.semaphore("s_g5b"))
        s_g6 = ctx.enter_context(nc.semaphore("s_g6"))
        s_g7 = ctx.enter_context(nc.semaphore("s_g7"))
        s_wdr = ctx.enter_context(nc.semaphore("s_wdr"))
        s_wb = ctx.enter_context(nc.semaphore("s_wb"))
        s_xe = ctx.enter_context(nc.semaphore("s_xe"))
        s_mmE = ctx.enter_context(nc.semaphore("s_mmE"))
        s_mm = ctx.enter_context(nc.semaphore("s_mm"))
        s_si = ctx.enter_context(nc.semaphore("s_si"))
        s_gst = ctx.enter_context(nc.semaphore("s_gst"))
        s_p6ls = [ctx.enter_context(nc.semaphore(f"s_p6ls{i}")) for i in range(2)]
        s_p6lg = ctx.enter_context(nc.semaphore("s_p6lg"))
        s_p6v = ctx.enter_context(nc.semaphore("s_p6v"))
        s_p6t = ctx.enter_context(nc.semaphore("s_p6t"))
        s_p6e = ctx.enter_context(nc.semaphore("s_p6e"))
        s_out = [ctx.enter_context(nc.semaphore(f"s_out{i}")) for i in range(3)]
        blk = ctx.enter_context(nc.Block())

        @blk.sync
        def _(sy):
            sy.dma_start(out=cf_dl[:], in_=dl_in[:, :]).then_inc(s_sm, 16)
            sy.dma_start(out=cf_al[:], in_=al_in[:, :]).then_inc(s_sm, 16)
            sy.dma_start(out=cf_bt[:], in_=bt_in[:, :]).then_inc(s_sm, 16)
            sy.dma_start(out=cf_gm[:], in_=gm_in[:, :]).then_inc(s_sm, 16)
            sy.dma_start(out=om_sb[:], in_=om_in[:, :]).then_inc(s_sm, 16)
            sy.dma_start(out=bg_sb[:], in_=bg_in[:, :]).then_inc(s_sm, 16)
            for i in range(NT):
                if i >= 2:
                    sy.wait_ge(s_ln, i - 1)
                sy.dma_start(out=x_tm[:, i % 2, :],
                             in_=x_slab[i * 128:(i + 1) * 128, :]
                             ).then_inc(s_x[i % 2], 16)
            for i in range(NT):
                sy.wait_ge(s_ln, NT)
                if i >= 2:
                    sy.wait_ge(s_nrm, i - 1)
                sy.dma_start(out=x_tm[:, i % 2, :],
                             in_=x_slab[i * 128:(i + 1) * 128, :]
                             ).then_inc(s_x[i % 2], 16)
                if i >= 1:
                    ip = i - 1
                    sy.wait_ge(s_ev, 8 * (ip + 1))
                    sy.dma_start(out=xn_snd.rearrange("kc p t -> p kc t")
                                 [:, :, ip * 128:(ip + 1) * 128],
                                 in_=xn_pack[:, :, :]).then_inc(s_snd1, 16)
            ip = NT - 1
            sy.wait_ge(s_ev, 8 * (ip + 1))
            sy.dma_start(out=xn_snd.rearrange("kc p t -> p kc t")
                         [:, :, ip * 128:(ip + 1) * 128],
                         in_=xn_pack[:, :, :]).then_inc(s_snd1, 16)
            for tc in range(TCN):
                if tc >= 1:
                    sy.wait_ge(s_g1, tc)
                sy.dma_start(out=xn_ch[:, :, :],
                             in_=xn_snd.rearrange("kc p t -> p kc t")
                             [:, :, tc * TCW:(tc + 1) * TCW]).then_inc(s_xch, 16)
            sy.wait_ge(s_g7, NT)
            sy.dma_start(out=w_dram.ap(), in_=w_emb[:, :]).then_inc(s_wdr, 16)
            for tc in range(TCN):
                if tc == 0:
                    sy.wait_ge(s_g1, TCN)
                else:
                    sy.wait_ge(s_xe, 8 * tc)
                sy.dma_start(out=xn_ch[:, :, :],
                             in_=xn_snd.rearrange("kc p t -> p kc t")
                             [:, :, tc * TCW:(tc + 1) * TCW]).then_inc(s_xch, 16)
            for tt in range(NT):
                if tt == 0:
                    sy.wait_ge(s_xe, 8 * TCN)
                else:
                    sy.wait_ge(s_p6v, 16 * tt)
                sy.dma_start(out=xn_ch[:, :, 0:128],
                             in_=xn_snd.rearrange("kc p t -> p kc t")
                             [:, :, tt * 128:(tt + 1) * 128]).then_inc(s_xch, 16)
                if tt >= 2:
                    sy.wait_ge(s_p6e, 8 * (tt - 1))
                else:
                    sy.wait_ge(s_nrm, NT)
                sy.dma_start(out=x_tm[:, tt % 2, :],
                             in_=x_slab[tt * 128:(tt + 1) * 128, :]
                             ).then_inc(s_x[tt % 2], 16)
                for i in range(KC):
                    idx = tt * KC + i
                    sy.wait_ge(s_p6e, idx + 1)
                    sy.dma_start(out=out_slab[tt * 128:(tt + 1) * 128,
                                              i * 128:(i + 1) * 128],
                                 in_=o_stage[:, idx % 3, :]
                                 ).then_inc(s_out[idx % 3], 16)
            for r in range(3):
                n_r = len([k for k in range(NT * KC) if k % 3 == r])
                sy.wait_ge(s_out[r], 16 * n_r)

        @blk.gpsimd
        def _(g):
            for e in range(E):
                g.dma_start(out=we_sb[:, e, :, :],
                            in_=we_t[e].rearrange("(kc p) h -> p kc h", p=128)
                            ).then_inc(s_we[e], 16)
            g.memset(ident[:, :], 0.0)
            g.affine_select(out=ident[:, :], in_=ident[:, :],
                            compare_op=OP.not_equal, fill=1.0, base=0,
                            pattern=[[-1, 128]], channel_multiplier=1
                            ).then_inc(s_id, 1)
            g.dma_start(out=be_sb[:, :], in_=be_in[:, :]).then_inc(s_wsm, 16)
            g.dma_start(out=wg_sb[:],
                        in_=wg_t.rearrange("(kc p) e -> p kc e", p=128)
                        ).then_inc(s_wsm, 16)
            g.wait_ge(s_snd1, 16 * NT)
            g.collective_compute(
                "AllToAll", OP.bypass,
                replica_groups=[list(range(CORES))],
                ins=[xn_snd_t.ap().opt()], outs=[xn_rcv_t.ap().opt()],
            ).then_inc(s_cc1, 1)
            g.wait_ge(s_cc1, 1)
            for w in range(NWIN):
                if w >= 1:
                    g.wait_ge(s_cst, w)
                g.dma_start(out=xs_bf[:, :],
                            in_=xn_rcv[w // 4, :, (w % 4) * WIN:(w % 4 + 1) * WIN]
                            ).then_inc(s_win, 16)
                if w >= 1:
                    wp = w - 1
                    g.wait_ge(s_scn, wp + 1)
                    g.dma_start(out=s_snd[wp // 4, :, (wp % 4) * WIN:(wp % 4 + 1) * WIN],
                                in_=sacc[:, wp % 2, :]).then_inc(s_ssnd[wp % 2], 16)
            wp = NWIN - 1
            g.wait_ge(s_scn, wp + 1)
            g.dma_start(out=s_snd[wp // 4, :, (wp % 4) * WIN:(wp % 4 + 1) * WIN],
                        in_=sacc[:, wp % 2, :]).then_inc(s_ssnd[wp % 2], 16)
            g.wait_ge(s_ssnd[0], 16 * (NWIN // 2))
            g.wait_ge(s_ssnd[1], 16 * (NWIN // 2))
            g.collective_compute(
                "AllToAll", OP.bypass,
                replica_groups=[list(range(CORES))],
                ins=[s_snd_t.ap().opt()], outs=[s_rcv_t.ap().opt()],
            ).then_inc(s_cc2, 1)
            g.wait_ge(s_cc2, 1)
            for tt in range(NT):
                if tt >= 2:
                    g.wait_ge(s_p6v, 16 * (tt - 1))
                g.dma_start(out=s_tile[:, tt % 2, :, :],
                            in_=s_rcv.rearrange("i ch t -> ch i t")
                            [:, :, tt * 128:(tt + 1) * 128]
                            ).then_inc(s_p6ls[tt % 2], 16)
                g.wait_ge(s_gst, 16 * 8 * (tt // 4 + 1))
                if tt >= 1:
                    g.wait_ge(s_p6v, 16 * tt)
                g.dma_start(out=gq_tile[:, :, :],
                            in_=g_stage.ap().rearrange("(dd p) t -> p dd t", p=128)
                            [:, :, tt * 128:(tt + 1) * 128]).then_inc(s_p6lg, 16)

        @blk.scalar
        def _(sc):
            sc.wait_ge(s_sm, 16 * 6)
            sc.activation(out=cf_p[:], in_=cf_dl[:], func=AF.Sigmoid)
            sc.activation(out=cf_a[:], in_=cf_al[:], func=AF.Sigmoid).then_inc(s_cf1, 1)
            sc.wait_ge(s_st, 1)
            sc.activation(out=st_sum[:, :], in_=st_var[:, :],
                          func=AF.Sqrt, bias=eps_c[:, 0:1]).then_inc(s_sq, 1)
            for q in range(4):
                sc.wait_ge(s_g5, q + 1)
                sc.activation(out=g_w1[:, 4 * q:4 * q + 4],
                              in_=g_d12[:, 4 * q:4 * q + 4],
                              func=AF.Sigmoid).then_inc(s_sg, 1)
            sc.activation(out=g_dt[:, 0:128], in_=ident[:, 0:128],
                          func=AF.Silu)
            sc.wait_ge(s_wdr, 16)
            for tc in range(min(2, TCN)):
                for e in range(E):
                    sc.dma_start(out=wb_sb[:, tc % 2, e, :],
                                 in_=w_dram[e:e + 1, tc * TCW:(tc + 1) * TCW]
                                 .broadcast_to([128, TCW])).then_inc(s_wb, 16)
            for tc in range(TCN):
                for dout in range(8):
                    k = tc * 8 + dout
                    sc.wait_ge(s_mm, 8 * (tc + 1))
                    if k >= 1:
                        sc.wait_ge(s_gst, 16 * k)
                    sc.activation(out=g_dt[:, :], in_=ps[:, dout, :],
                                  func=AF.Silu).then_inc(s_si, 1)
                    sc.dma_start(out=g_stage.ap()
                                 .rearrange("(dd p) t -> p dd t", p=128)
                                 [:, dout, tc * TCW:(tc + 1) * TCW],
                                 in_=g_dt[:, :]).then_inc(s_gst, 16)
                if tc + 2 < TCN:
                    sc.wait_ge(s_xe, 8 * (tc + 1))
                    for e in range(E):
                        sc.dma_start(out=wb_sb[:, tc % 2, e, :],
                                     in_=w_dram[e:e + 1,
                                                (tc + 2) * TCW:(tc + 3) * TCW]
                                     .broadcast_to([128, TCW])).then_inc(s_wb, 16)

        @blk.tensor
        def _(t):
            t.wait_ge(s_id, 1)
            for i in range(NT):
                t.wait_ge(s_nrm, i + 1)
                for kc in range(KC):
                    idx = i * KC + kc
                    if idx >= 2:
                        t.wait_ge(s_ev, idx - 1)
                    t.transpose(ps[:, 1 + idx % 2, 0:128],
                                xn_f32[:, kc * 128:(kc + 1) * 128],
                                ident[:, :]).then_inc(s_tp, 1)
            t.wait_ge(s_wsm, 32)
            for q in range(TCN):
                t.wait_ge(s_xch, 16 * (q + 1))
                if q >= 1:
                    t.wait_ge(s_g2, q)
                for kc in range(KC):
                    t.matmul(ps[0:8, 0, :], wg_sb[:, kc, :],
                             xn_ch[:, kc, :],
                             start=(kc == 0), stop=(kc == KC - 1)
                             ).then_inc(s_g1, 1 if kc == KC - 1 else 0)
                t.wait_ge(s_g2, q + 1)
                for j in range(4):
                    tt_i = 4 * q + j
                    if tt_i >= 2:
                        t.wait_ge(s_g4, tt_i - 1)
                    t.transpose(ps[0:128, 1 + tt_i % 2, 0:8],
                                lg_em[0:8, j * 128:(j + 1) * 128],
                                ident[0:8, 0:8]).then_inc(s_g3, 1)
            for q in range(4):
                t.wait_ge(s_g5b, q + 1)
                for j in range(4):
                    tt_i = 4 * q + j
                    if tt_i >= 2:
                        t.wait_ge(s_g7, tt_i - 1)
                    t.transpose(ps[0:8, 4 + tt_i % 2, 0:128],
                                w_tm[:, tt_i, :],
                                ident[:, :]).then_inc(s_g6, 1)
            t.wait_ge(s_g7, NT)
            for tc in range(TCN):
                if tc >= 1:
                    t.wait_ge(s_si, 8 * tc)
                for e in range(E):
                    t.wait_ge(s_xe, tc * 8 + e + 1)
                    t.wait_ge(s_we[e], 16)
                    ins = None
                    for dout in range(8):
                        for kc in range(KC):
                            ins = t.matmul(
                                ps[:, dout, :],
                                we_sb[:, e, kc, dout * 128:(dout + 1) * 128],
                                xn_eb[:, kc, :],
                                start=(e == 0 and kc == 0), stop=False)
                    ins.then_inc(s_mmE, 1)
                for dout in range(8):
                    t.matmul(ps[:, dout, :], be_sb[0:8, dout * 128:(dout + 1) * 128],
                             w_emb[0:8, tc * TCW:(tc + 1) * TCW],
                             start=False, stop=True).then_inc(s_mm, 1)
            t.wait_ge(s_si, 8 * TCN)
            for idx in range(NT * KC):
                t.wait_ge(s_p6v, 2 * (idx + 1))
                if idx >= 2:
                    t.wait_ge(s_p6e, idx - 1)
                t.transpose(ps[:, 1 + idx % 2, 0:128],
                            t2_sb[:, idx % 2, :], ident[:, :]).then_inc(s_p6t, 1)

        @blk.vector
        def _(v):
            v.memset(eps_c[:, :], EPS)

            def spacer():
                v.tensor_copy(out=xs_f[:, 0:128], in_=ident[:, 0:128])

            for i in range(NT):
                v.wait_ge(s_x[i % 2], 16 * (i // 2 + 1))
                v.reduce_sum(st_sum[:, i:i + 1], x_tm[:, i % 2, :], axis=AX)
                v.tensor_tensor(out=scr[:], in0=x_tm[:, i % 2, :],
                                in1=x_tm[:, i % 2, :], op=OP.mult)
                v.reduce_sum(st_ex2[:, i:i + 1], scr[:], axis=AX
                             ).then_inc(s_ln, 1)
            spacer()
            v.tensor_scalar(out=st_nmu[:, :], in0=st_sum[:, :],
                            scalar1=-1.0 / D, scalar2=None, op0=OP.mult)
            spacer()
            v.tensor_scalar(out=st_ex2[:, :], in0=st_ex2[:, :],
                            scalar1=1.0 / D, scalar2=None, op0=OP.mult)
            spacer()
            v.tensor_tensor(out=st_var[:, :], in0=st_nmu[:, :], in1=st_nmu[:, :],
                            op=OP.mult)
            spacer()
            v.tensor_tensor(out=st_var[:, :], in0=st_ex2[:, :], in1=st_var[:, :],
                            op=OP.subtract)
            v.tensor_copy(out=xs_f[:, 0:128], in_=ident[:, 0:128]
                          ).then_inc(s_st, 1)
            v.wait_ge(s_sq, 1)
            spacer()
            v.reciprocal(out=st_r[:, :], in_=st_sum[:, :])
            v.tensor_scalar(out=st_ex2[:, :], in0=st_var[:, :],
                            scalar1=EPS, scalar2=None, op0=OP.add)
            spacer()
            v.tensor_tensor(out=st_aux[:, :], in0=st_r[:, :], in1=st_r[:, :],
                            op=OP.mult)
            spacer()
            v.tensor_tensor(out=st_aux[:, :], in0=st_aux[:, :], in1=st_ex2[:, :],
                            op=OP.mult)
            spacer()
            v.tensor_scalar(out=st_aux[:, :], in0=st_aux[:, :],
                            scalar1=-0.5, scalar2=1.5, op0=OP.mult, op1=OP.add)
            spacer()
            v.tensor_tensor(out=st_r[:, :], in0=st_r[:, :], in1=st_aux[:, :],
                            op=OP.mult)
            spacer()
            v.wait_ge(s_cf1, 1)
            v.wait_ge(s_sm, 16 * 6)
            spacer()
            v.tensor_tensor(out=cf_q[:], in0=cf_p[:], in1=cf_a[:], op=OP.mult)
            v.tensor_tensor(out=st_sum[:, :], in0=cf_bt[:], in1=cf_gm[:],
                            op=OP.mult)
            v.tensor_scalar(out=st_aux[:, :], in0=cf_p[:],
                            scalar1=0.25, scalar2=None, op0=OP.mult)
            v.tensor_scalar(out=cf_q[:], in0=cf_q[:], scalar1=-1.0, scalar2=1.0,
                            op0=OP.mult, op1=OP.add)
            v.tensor_tensor(out=cf_c[:], in0=st_sum[:, :], in1=st_aux[:, :],
                            op=OP.mult)
            spacer()
            v.tensor_copy(out=st_aux[:, 0:1], in_=cf_c[:, 0:1]).then_inc(s_cf2, 1)
            for i in range(NT):
                v.wait_ge(s_x[i % 2], 16 * ((NT + i) // 2 + 1))
                if i >= 1:
                    v.wait_ge(s_tp, 8 * i)
                v.tensor_scalar(out=xn_f32[:, :], in0=x_tm[:, i % 2, :],
                                scalar1=st_nmu[:, i:i + 1],
                                scalar2=st_r[:, i:i + 1],
                                op0=OP.add, op1=OP.mult)
                v.tensor_copy(out=xs_f[:, 0:128], in_=ident[:, 0:128]
                              ).then_inc(s_nrm, 1)
                for kc in range(KC):
                    idx = i * KC + kc
                    if kc == 0 and i >= 1:
                        v.wait_ge(s_snd1, 16 * i)
                    v.wait_ge(s_tp, idx + 1)
                    v.tensor_copy(out=xn_pack[:, kc, :],
                                  in_=ps[:, 1 + idx % 2, 0:128]
                                  ).then_inc(s_ev, 1)
            for q in range(TCN):
                v.wait_ge(s_g1, q + 1)
                if q >= 1:
                    v.wait_ge(s_g3, 4 * q)
                v.tensor_scalar(out=lg_em[0:8, :], in0=ps[0:8, 0, :],
                                scalar1=bg_sb[0:8, 0:1], scalar2=None,
                                op0=OP.add)
                v.tensor_copy(out=xs_f[:, 0:128], in_=ident[:, 0:128]
                              ).then_inc(s_g2, 1)
                for j in range(4):
                    tt_i = 4 * q + j
                    v.wait_ge(s_g3, tt_i + 1)
                    v.tensor_copy(out=lg_tm[:, tt_i, :],
                                  in_=ps[0:128, 1 + tt_i % 2, 0:8]
                                  ).then_inc(s_g4, 1)
                tls = list(range(4 * q, 4 * q + 4))
                for ti in tls:
                    v.reduce_max(g_m1[:, ti:ti + 1], lg_tm[:, ti, :], axis=AX)
                for ti in tls:
                    v.tensor_scalar(out=g_eq1[:, ti % 4, :], in0=lg_tm[:, ti, :],
                                    scalar1=g_m1[:, ti:ti + 1], scalar2=None,
                                    op0=OP.is_equal)
                for ti in tls:
                    v.scalar_tensor_tensor(out=g_x1[:, ti % 4, :],
                                           in0=g_eq1[:, ti % 4, :],
                                           scalar=-1e30, in1=lg_tm[:, ti, :],
                                           op0=OP.mult, op1=OP.add)
                for ti in tls:
                    v.reduce_max(g_m2[:, ti:ti + 1], g_x1[:, ti % 4, :], axis=AX)
                for ti in tls:
                    v.tensor_scalar(out=g_eq2[:, ti % 4, :], in0=g_x1[:, ti % 4, :],
                                    scalar1=g_m2[:, ti:ti + 1], scalar2=None,
                                    op0=OP.is_equal)
                v.tensor_tensor(out=g_d12[:, 4 * q:4 * q + 4],
                                in0=g_m1[:, 4 * q:4 * q + 4],
                                in1=g_m2[:, 4 * q:4 * q + 4],
                                op=OP.subtract).then_inc(s_g5, 1)
                v.wait_ge(s_sg, q + 1)
                spacer()
                v.tensor_scalar(out=g_w2[:, 4 * q:4 * q + 4],
                                in0=g_w1[:, 4 * q:4 * q + 4],
                                scalar1=-1.0, scalar2=1.0, op0=OP.mult, op1=OP.add)
                for ti in tls:
                    v.tensor_scalar(out=g_x1[:, ti % 4, :], in0=g_eq1[:, ti % 4, :],
                                    scalar1=g_w1[:, ti:ti + 1], scalar2=None,
                                    op0=OP.mult)
                for ti in tls:
                    v.scalar_tensor_tensor(out=w_tm[:, ti, :],
                                           in0=g_eq2[:, ti % 4, :],
                                           scalar=g_w2[:, ti:ti + 1],
                                           in1=g_x1[:, ti % 4, :],
                                           op0=OP.mult, op1=OP.add)
                spacer()
                v.tensor_copy(out=xs_f[:, 0:128], in_=ident[:, 0:128]
                              ).then_inc(s_g5b, 1)
            for tt_i in range(NT):
                v.wait_ge(s_g6, tt_i + 1)
                v.tensor_copy(out=w_emb[0:8, tt_i * 128:(tt_i + 1) * 128],
                              in_=ps[0:8, 4 + tt_i % 2, 0:128]
                              ).then_inc(s_g7, 1)
            scan_work = []

            def emit_scans(budget):
                n = 0
                while scan_work and n < budget:
                    scan_work.pop(0)()
                    n += 1

            for w in range(NWIN):
                def cast(w=w):
                    v.wait_ge(s_win, 16 * (w + 1))
                    if w >= 2:
                        v.wait_ge(s_ssnd[w % 2], 16 * (w // 2))
                    if w == 0:
                        v.wait_ge(s_cf2, 1)
                    v.tensor_copy(out=xs_f[:, :], in_=xs_bf[:, :]
                                  ).then_inc(s_cst, 1)
                scan_work.append(cast)
                for n in range(NS):
                    def scan(w=w, n=n):
                        q_b = cf_q[:, n:n + 1].broadcast_to([128, WIN])
                        if w % WPB == 0:
                            v.tensor_tensor_scan(out=h_w[:, :], data0=q_b,
                                                 data1=xs_f[:, :], initial=0.0,
                                                 op0=OP.mult, op1=OP.add)
                        else:
                            v.tensor_tensor_scan(out=h_w[:, :], data0=q_b,
                                                 data1=xs_f[:, :],
                                                 initial=carry[:, n:n + 1],
                                                 op0=OP.mult, op1=OP.add)

                    def comb(w=w, n=n):
                        if n == 0:
                            v.tensor_scalar(out=sacc[:, w % 2, :], in0=h_w[:, :],
                                            scalar1=cf_c[:, 0:1], scalar2=None,
                                            op0=OP.mult)
                        else:
                            v.scalar_tensor_tensor(out=sacc[:, w % 2, :],
                                                   in0=h_w[:, :],
                                                   scalar=cf_c[:, n:n + 1],
                                                   in1=sacc[:, w % 2, :],
                                                   op0=OP.mult, op1=OP.add)

                    def carrycp(w=w, n=n):
                        ins = v.tensor_copy(out=carry[:, n:n + 1],
                                            in_=h_w[:, WIN - 1:WIN])
                        if n == NS - 1:
                            ins.then_inc(s_scn, 1)
                    scan_work.append(scan)
                    scan_work.append(comb)
                    scan_work.append(carrycp)

            per_slot = (len(scan_work) + TCN * E - 1) // (TCN * E)
            for tc in range(TCN):
                for e in range(E):
                    gi = tc * 8 + e
                    if gi >= 1:
                        v.wait_ge(s_mmE, gi)
                    v.wait_ge(s_wb, 256 if tc < 2 else 256 + 128 * (tc - 1))
                    v.wait_ge(s_xch, 16 * (TCN + tc + 1))
                    v.tensor_tensor(
                        out=xn_eb[:, :, :],
                        in0=xn_ch[:, :, :],
                        in1=wb_sb[:, tc % 2, e, :]
                        .rearrange("p (a t) -> p a t", a=1)
                        .broadcast_to([128, KC, TCW]),
                        op=OP.mult)
                    v.tensor_copy(out=t1_sb[:, :], in_=ident[:, :])
                    v.tensor_copy(out=t1_sb[:, :], in_=ident[:, :]
                                  ).then_inc(s_xe, 1)
                    emit_scans(per_slot)
            emit_scans(10 ** 9)
            def p6_evac(ix, also_p6v):
                v.wait_ge(s_p6t, ix + 1)
                if ix >= 3:
                    v.wait_ge(s_out[ix % 3], 16 * (ix // 3))
                v.tensor_tensor(out=o_stage[:, ix % 3, :],
                                in0=ps[:, 1 + ix % 2, 0:128],
                                in1=x_tm[:, (ix // KC) % 2,
                                         (ix % KC) * 128:(ix % KC) * 128 + 128],
                                op=OP.add).then_inc(s_p6e, 1)
                if also_p6v:
                    v.tensor_copy(out=xs_f[:, 0:8], in_=ident[:, 0:8]
                                  ).then_inc(s_p6v, 2)

            for idx in range(NT * KC):
                tt, i = divmod(idx, KC)
                if i == 0 and idx >= 1:
                    p6_evac(idx - 1, False)
                if i == 0:
                    v.wait_ge(s_p6ls[tt % 2], 16 * (tt // 2 + 1))
                    v.wait_ge(s_p6lg, 16 * (tt + 1))
                    v.wait_ge(s_xch, 16 * (2 * TCN + tt + 1))
                    v.wait_ge(s_x[tt % 2], 16 * ((2 * NT + tt) // 2 + 1))
                v.scalar_tensor_tensor(out=t1_sb[:, :],
                                       in0=xn_ch[:, i, 0:128],
                                       scalar=om_sb[:, i:i + 1],
                                       in1=s_tile[:, tt % 2, i, :],
                                       op0=OP.mult, op1=OP.add)
                v.tensor_tensor(out=t2_sb[:, idx % 2, :], in0=t1_sb[:, :],
                                in1=gq_tile[:, i, :], op=OP.mult)
                if i == 0:
                    v.tensor_copy(out=xs_f[:, 0:128], in_=ident[:, 0:128]
                                  ).then_inc(s_p6v, 2)
                else:
                    p6_evac(idx - 1, True)
            p6_evac(NT * KC - 1, False)
    return nc


_NC_CACHE = {}


def _get_nc():
    if "nc" not in _NC_CACHE:
        _NC_CACHE["nc"] = build_nc()
    return _NC_CACHE["nc"]


def kernel(L0_attn_x, L0_attn_delta, L0_attn_alpha, L0_attn_beta, L0_attn_gamma,
           L0_attn_omega, L0_ffn_We, L0_ffn_be, L0_ffn_Wg, L0_ffn_bg):
    x = np.ascontiguousarray(np.asarray(L0_attn_x, dtype=np.float32)
                             .reshape(16384, 1024))
    we_t = np.ascontiguousarray(np.asarray(L0_ffn_We, np.float32).transpose(0, 2, 1))
    wg_t = np.ascontiguousarray(np.asarray(L0_ffn_Wg, np.float32).T)
    be = np.ascontiguousarray(np.asarray(L0_ffn_be, np.float32))
    bg = np.ascontiguousarray(np.asarray(L0_ffn_bg, np.float32).reshape(8, 1))
    om = np.ascontiguousarray(np.asarray(L0_attn_omega, np.float32)
                              .reshape(8, 128).T)
    dl = np.asarray(L0_attn_delta, np.float32)[:, :, 0]
    al = np.asarray(L0_attn_alpha, np.float32)[:, :, 0]
    bt = np.asarray(L0_attn_beta, np.float32)[:, :, 0]
    gm = np.asarray(L0_attn_gamma, np.float32)

    in_maps = []
    for j in range(CORES):
        sl = slice(j * TOK, (j + 1) * TOK)
        ch = slice(j * 128, (j + 1) * 128)
        in_maps.append({
            "x_slab": np.ascontiguousarray(x[sl]),
            "we_t": we_t, "wg_t": wg_t, "be_in": be, "bg_in": bg, "om_in": om,
            "dl_in": np.ascontiguousarray(dl[ch]),
            "al_in": np.ascontiguousarray(al[ch]),
            "bt_in": np.ascontiguousarray(bt[ch]),
            "gm_in": np.ascontiguousarray(gm[ch]),
        })
    nc = _get_nc()
    res = run_bass_kernel_spmd(nc, in_maps, core_ids=list(range(CORES)))
    out = np.concatenate([r["out_slab"] for r in res.results], axis=0)
    return out.reshape(2, 8192, 1024)


# version 06
# speedup vs baseline: 1.1009x; 1.0413x over previous
import os
import numpy as np
import concourse.bass as bass
import concourse.mybir as mybir
from concourse.bass_utils import run_bass_kernel_spmd

os.environ.setdefault("TRN_ALLOW_ICI_ALL_TO_ALL", "1")

F32 = mybir.dt.float32
BF16 = mybir.dt.bfloat16
OP = mybir.AluOpType
AF = mybir.ActivationFunctionType
AX = mybir.AxisListType.X

CORES = 8
TOK = 2048
D = 1024
E = 8
NS = 16
NT = TOK // 128
KC = 8
TCN = 8
TCW = 256
TPC = TCW // 128
WIN = 512
NWIN = (CORES * TOK) // WIN
WPB = NWIN // 2
EPS = 1e-5


def build_nc():
    nc = bass.Bass("TRN2", target_bir_lowering=False, debug=False,
                   detect_race_conditions=False)

    x_slab = nc.dram_tensor("x_slab", [TOK, D], F32, kind="ExternalInput").ap()
    we_t = nc.dram_tensor("we_t", [E, D, D], F32, kind="ExternalInput").ap()
    wg_t = nc.dram_tensor("wg_t", [D, E], F32, kind="ExternalInput").ap()
    be_in = nc.dram_tensor("be_in", [E, D], F32, kind="ExternalInput").ap()
    bg_in = nc.dram_tensor("bg_in", [E, 1], F32, kind="ExternalInput").ap()
    om_in = nc.dram_tensor("om_in", [128, KC], F32, kind="ExternalInput").ap()
    dl_in = nc.dram_tensor("dl_in", [128, NS], F32, kind="ExternalInput").ap()
    al_in = nc.dram_tensor("al_in", [128, NS], F32, kind="ExternalInput").ap()
    bt_in = nc.dram_tensor("bt_in", [128, NS], F32, kind="ExternalInput").ap()
    gm_in = nc.dram_tensor("gm_in", [128, NS], F32, kind="ExternalInput").ap()
    out_slab = nc.dram_tensor("out_slab", [TOK, D], F32, kind="ExternalOutput").ap()

    xn_snd_t = nc.dram_tensor("xn_snd", [CORES, 128, TOK], BF16)
    xn_rcv_t = nc.dram_tensor("xn_rcv", [CORES, 128, TOK], BF16)
    s_snd_t = nc.dram_tensor("s_snd", [CORES, 128, TOK], BF16)
    s_rcv_t = nc.dram_tensor("s_rcv", [CORES, 128, TOK], BF16)
    g_stage = nc.dram_tensor("g_stage", [D, TOK], BF16)
    w_dram = nc.dram_tensor("w_dram", [E, TOK], BF16)
    qc_dram = nc.dram_tensor("qc_dram", [2, 128, NS], F32)
    xn_snd = xn_snd_t.ap()
    xn_rcv = xn_rcv_t.ap()
    s_snd = s_snd_t.ap()
    s_rcv = s_rcv_t.ap()

    from contextlib import ExitStack
    with ExitStack() as ctx:
        ctx.enter_context(nc.reset_on_exit())
        we_sb = ctx.enter_context(nc.sbuf_tensor("we_sb", [128, E, KC, D], BF16))
        xn_ch = ctx.enter_context(nc.sbuf_tensor("xn_ch", [128, KC, TCW], BF16))
        xn_eb = ctx.enter_context(nc.sbuf_tensor("xn_eb", [128, KC, TCW], BF16))
        wb_sb = ctx.enter_context(nc.sbuf_tensor("wb_sb", [128, 2, E, TCW], BF16))
        g_dt = ctx.enter_context(nc.sbuf_tensor("g_dt", [128, TCW], BF16))
        x_tm = ctx.enter_context(nc.sbuf_tensor("x_tm", [128, 2, D], F32))
        st_nmu = ctx.enter_context(nc.sbuf_tensor("st_nmu", [128, NT], F32))
        st_ex2 = ctx.enter_context(nc.sbuf_tensor("st_ex2", [128, NT], F32))
        st_var = ctx.enter_context(nc.sbuf_tensor("st_var", [128, NT], F32))
        st_r = ctx.enter_context(nc.sbuf_tensor("st_r", [128, NT], F32))
        st_sum = ctx.enter_context(nc.sbuf_tensor("st_sum", [128, NT], F32))
        st_aux = ctx.enter_context(nc.sbuf_tensor("st_aux", [128, NT], F32))
        wg_sb = ctx.enter_context(nc.sbuf_tensor("wg_sb", [128, KC, E], BF16))
        be_sb = ctx.enter_context(nc.sbuf_tensor("be_sb", [8, D], BF16))
        bg_sb = ctx.enter_context(nc.sbuf_tensor("bg_sb", [8, 1], F32))
        om_sb = ctx.enter_context(nc.sbuf_tensor("om_sb", [128, KC], F32))
        cf_dl = ctx.enter_context(nc.sbuf_tensor("cf_dl", [128, NS], F32))
        cf_al = ctx.enter_context(nc.sbuf_tensor("cf_al", [128, NS], F32))
        cf_bt = ctx.enter_context(nc.sbuf_tensor("cf_bt", [128, NS], F32))
        cf_gm = ctx.enter_context(nc.sbuf_tensor("cf_gm", [128, NS], F32))
        cf_p = ctx.enter_context(nc.sbuf_tensor("cf_p", [128, NS], F32))
        cf_a = ctx.enter_context(nc.sbuf_tensor("cf_a", [128, NS], F32))
        cf_q = ctx.enter_context(nc.sbuf_tensor("cf_q", [128, NS], F32))
        cf_c = ctx.enter_context(nc.sbuf_tensor("cf_c", [128, NS], F32))
        ident = ctx.enter_context(nc.sbuf_tensor("ident", [128, 128], F32))
        lg_em = ctx.enter_context(nc.sbuf_tensor("lg_em", [8, TCW], F32))
        lg_tm = ctx.enter_context(nc.sbuf_tensor("lg_tm", [128, NT, E], F32))
        g_m1 = ctx.enter_context(nc.sbuf_tensor("g_m1", [128, NT], F32))
        g_m2 = ctx.enter_context(nc.sbuf_tensor("g_m2", [128, NT], F32))
        g_d12 = ctx.enter_context(nc.sbuf_tensor("g_d12", [128, NT], F32))
        g_w1 = ctx.enter_context(nc.sbuf_tensor("g_w1", [128, NT], F32))
        g_w2 = ctx.enter_context(nc.sbuf_tensor("g_w2", [128, NT], F32))
        g_eq1 = ctx.enter_context(nc.sbuf_tensor("g_eq1", [128, 4, E], F32))
        g_eq2 = ctx.enter_context(nc.sbuf_tensor("g_eq2", [128, 4, E], F32))
        g_x1 = ctx.enter_context(nc.sbuf_tensor("g_x1", [128, 4, E], F32))
        w_tm = ctx.enter_context(nc.sbuf_tensor("w_tm", [128, NT, E], F32))
        w_emb = ctx.enter_context(nc.sbuf_tensor("w_emb", [8, TOK], BF16))
        xs_rep = ctx.enter_context(nc.sbuf_tensor("xs_rep", [128, 2, WIN], BF16))
        xs_f = ctx.enter_context(nc.sbuf_tensor("xs_f", [128, WIN], F32))
        h_pk = ctx.enter_context(nc.sbuf_tensor("h_pk", [128, NS, WIN], BF16))
        spk = ctx.enter_context(nc.sbuf_tensor("spk", [64, 2, 2, WIN], BF16))
        q_pk = ctx.enter_context(nc.sbuf_tensor("q_pk", [128, NS], F32))
        c_pk = ctx.enter_context(nc.sbuf_tensor("c_pk", [128, NS], F32))
        cmat = ctx.enter_context(nc.sbuf_tensor("cmat", [128, NS, 64], BF16))
        bmask = ctx.enter_context(nc.sbuf_tensor("bmask", [128, 64], F32))
        car_pk = ctx.enter_context(nc.sbuf_tensor("car_pk", [128, NS], F32))
        scrb = ctx.enter_context(nc.sbuf_tensor("scrb", [128, D], F32))
        scr = scrb.ap()
        eps_c = ctx.enter_context(nc.sbuf_tensor("eps_c", [128, 1], F32))
        s_tile = ctx.enter_context(nc.sbuf_tensor("s_tile", [128, 2, KC, 128], BF16))
        gq_tile = ctx.enter_context(nc.sbuf_tensor("gq_tile", [128, KC, 128], BF16))
        t1_sb = ctx.enter_context(nc.sbuf_tensor("t1_sb", [128, 128], F32))
        t2_sb = ctx.enter_context(nc.sbuf_tensor("t2_sb", [128, 2, 128], F32))
        o_stage = ctx.enter_context(nc.sbuf_tensor("o_stage", [128, 3, 128], F32))
        ps = ctx.enter_context(nc.psum_tensor("ps", [128, 8, 512], F32))
        s_sm = ctx.enter_context(nc.semaphore("s_sm"))
        s_cst = ctx.enter_context(nc.semaphore("s_cst"))
        s_x = [ctx.enter_context(nc.semaphore(f"s_x{i}")) for i in range(2)]
        s_we = [ctx.enter_context(nc.semaphore(f"s_we{i}")) for i in range(E)]
        s_wsm = ctx.enter_context(nc.semaphore("s_wsm"))
        s_id = ctx.enter_context(nc.semaphore("s_id"))
        s_cf1 = ctx.enter_context(nc.semaphore("s_cf1"))
        s_cf2 = ctx.enter_context(nc.semaphore("s_cf2"))
        s_ln = ctx.enter_context(nc.semaphore("s_ln"))
        s_st = ctx.enter_context(nc.semaphore("s_st"))
        s_sq = ctx.enter_context(nc.semaphore("s_sq"))
        s_nrm = ctx.enter_context(nc.semaphore("s_nrm"))
        s_tp = ctx.enter_context(nc.semaphore("s_tp"))
        s_ev = ctx.enter_context(nc.semaphore("s_ev"))
        s_snd1 = ctx.enter_context(nc.semaphore("s_snd1"))
        s_cc1 = ctx.enter_context(nc.semaphore("s_cc1"))
        s_xch = ctx.enter_context(nc.semaphore("s_xch"))
        s_win = [ctx.enter_context(nc.semaphore(f"s_win{i}")) for i in range(2)]
        s_qcw = ctx.enter_context(nc.semaphore("s_qcw"))
        s_qcr = ctx.enter_context(nc.semaphore("s_qcr"))
        s_msk = ctx.enter_context(nc.semaphore("s_msk"))
        s_hrdy = ctx.enter_context(nc.semaphore("s_hrdy"))
        s_cmb = ctx.enter_context(nc.semaphore("s_cmb"))
        s_sevc = ctx.enter_context(nc.semaphore("s_sevc"))
        s_scn = ctx.enter_context(nc.semaphore("s_scn"))
        s_ssnd = [ctx.enter_context(nc.semaphore(f"s_ssnd{i}")) for i in range(2)]
        s_cc2 = ctx.enter_context(nc.semaphore("s_cc2"))
        s_g1 = ctx.enter_context(nc.semaphore("s_g1"))
        s_g2 = ctx.enter_context(nc.semaphore("s_g2"))
        s_g3 = ctx.enter_context(nc.semaphore("s_g3"))
        s_g4 = ctx.enter_context(nc.semaphore("s_g4"))
        s_sg = ctx.enter_context(nc.semaphore("s_sg"))
        s_g5 = ctx.enter_context(nc.semaphore("s_g5"))
        s_g5b = ctx.enter_context(nc.semaphore("s_g5b"))
        s_g6 = ctx.enter_context(nc.semaphore("s_g6"))
        s_g7 = ctx.enter_context(nc.semaphore("s_g7"))
        s_wdr = ctx.enter_context(nc.semaphore("s_wdr"))
        s_wb = ctx.enter_context(nc.semaphore("s_wb"))
        s_xe = ctx.enter_context(nc.semaphore("s_xe"))
        s_mmE = ctx.enter_context(nc.semaphore("s_mmE"))
        s_mm = ctx.enter_context(nc.semaphore("s_mm"))
        s_si = ctx.enter_context(nc.semaphore("s_si"))
        s_gst = ctx.enter_context(nc.semaphore("s_gst"))
        s_p6ls = [ctx.enter_context(nc.semaphore(f"s_p6ls{i}")) for i in range(2)]
        s_p6lg = ctx.enter_context(nc.semaphore("s_p6lg"))
        s_p6v = ctx.enter_context(nc.semaphore("s_p6v"))
        s_p6t = ctx.enter_context(nc.semaphore("s_p6t"))
        s_p6e = ctx.enter_context(nc.semaphore("s_p6e"))
        s_out = [ctx.enter_context(nc.semaphore(f"s_out{i}")) for i in range(3)]
        blk = ctx.enter_context(nc.Block())

        @blk.sync
        def _(sy):
            sy.dma_start(out=cf_dl[:], in_=dl_in[:, :]).then_inc(s_sm, 16)
            sy.dma_start(out=cf_al[:], in_=al_in[:, :]).then_inc(s_sm, 16)
            sy.dma_start(out=cf_bt[:], in_=bt_in[:, :]).then_inc(s_sm, 16)
            sy.dma_start(out=cf_gm[:], in_=gm_in[:, :]).then_inc(s_sm, 16)
            sy.dma_start(out=om_sb[:], in_=om_in[:, :]).then_inc(s_sm, 16)
            sy.dma_start(out=bg_sb[:], in_=bg_in[:, :]).then_inc(s_sm, 16)
            for i in range(NT):
                if i >= 2:
                    sy.wait_ge(s_ln, i - 1)
                sy.dma_start(out=x_tm[:, i % 2, :],
                             in_=x_slab[i * 128:(i + 1) * 128, :]
                             ).then_inc(s_x[i % 2], 16)
            for i in range(NT):
                sy.wait_ge(s_ln, NT)
                if i >= 2:
                    sy.wait_ge(s_nrm, 8 * (i - 1))
                sy.dma_start(out=x_tm[:, i % 2, :],
                             in_=x_slab[i * 128:(i + 1) * 128, :]
                             ).then_inc(s_x[i % 2], 16)
                if i >= 1:
                    ip = i - 1
                    sy.wait_ge(s_ev, 8 * (ip + 1))
                    sy.dma_start(out=xn_snd.rearrange("kc p t -> p kc t")
                                 [:, :, ip * 128:(ip + 1) * 128],
                                 in_=xn_eb[:, :, 0:128]).then_inc(s_snd1, 16)
            ip = NT - 1
            sy.wait_ge(s_ev, 8 * (ip + 1))
            sy.dma_start(out=xn_snd.rearrange("kc p t -> p kc t")
                         [:, :, ip * 128:(ip + 1) * 128],
                         in_=xn_eb[:, :, 0:128]).then_inc(s_snd1, 16)
            for tc in range(TCN):
                if tc >= 1:
                    sy.wait_ge(s_g1, tc)
                sy.dma_start(out=xn_ch[:, :, :],
                             in_=xn_snd.rearrange("kc p t -> p kc t")
                             [:, :, tc * TCW:(tc + 1) * TCW]).then_inc(s_xch, 16)
            sy.wait_ge(s_g7, NT)
            sy.dma_start(out=w_dram.ap(), in_=w_emb[:, :]).then_inc(s_wdr, 16)
            for tc in range(TCN):
                if tc == 0:
                    sy.wait_ge(s_g1, TCN)
                else:
                    sy.wait_ge(s_xe, 8 * tc)
                sy.dma_start(out=xn_ch[:, :, :],
                             in_=xn_snd.rearrange("kc p t -> p kc t")
                             [:, :, tc * TCW:(tc + 1) * TCW]).then_inc(s_xch, 16)
            for tt in range(NT):
                if tt == 0:
                    sy.wait_ge(s_xe, 8 * TCN)
                else:
                    sy.wait_ge(s_p6v, 16 * tt)
                sy.dma_start(out=xn_ch[:, :, 0:128],
                             in_=xn_snd.rearrange("kc p t -> p kc t")
                             [:, :, tt * 128:(tt + 1) * 128]).then_inc(s_xch, 16)
                if tt >= 2:
                    sy.wait_ge(s_p6e, 8 * (tt - 1))
                else:
                    sy.wait_ge(s_nrm, NT * KC)
                sy.dma_start(out=x_tm[:, tt % 2, :],
                             in_=x_slab[tt * 128:(tt + 1) * 128, :]
                             ).then_inc(s_x[tt % 2], 16)
                for i in range(KC):
                    idx = tt * KC + i
                    sy.wait_ge(s_p6e, idx + 1)
                    sy.dma_start(out=out_slab[tt * 128:(tt + 1) * 128,
                                              i * 128:(i + 1) * 128],
                                 in_=o_stage[:, idx % 3, :]
                                 ).then_inc(s_out[idx % 3], 16)
            for r in range(3):
                n_r = len([k for k in range(NT * KC) if k % 3 == r])
                sy.wait_ge(s_out[r], 16 * n_r)

        @blk.gpsimd
        def _(g):
            for e in range(E):
                g.dma_start(out=we_sb[:, e, :, :],
                            in_=we_t[e].rearrange("(kc p) h -> p kc h", p=128)
                            ).then_inc(s_we[e], 16)
            g.memset(ident[:, :], 0.0)
            g.affine_select(out=ident[:, :], in_=ident[:, :],
                            compare_op=OP.not_equal, fill=1.0, base=0,
                            pattern=[[-1, 128]], channel_multiplier=1
                            ).then_inc(s_id, 1)
            g.dma_start(out=be_sb[:, :], in_=be_in[:, :]).then_inc(s_wsm, 16)
            g.memset(bmask[:, :], 0.0)
            ins_m = None
            for half in range(2):
                ins_m = g.affine_select(
                    out=bmask[64 * half:64 * (half + 1), :],
                    in_=bmask[64 * half:64 * (half + 1), :],
                    compare_op=OP.not_equal, fill=1.0, base=0,
                    pattern=[[-1, 64]], channel_multiplier=1)
            ins_m.then_inc(s_msk, 1)
            g.wait_ge(s_cf2, 1)
            g.dma_start(out=qc_dram.ap()[0], in_=cf_q[:, :]).then_inc(s_qcw, 16)
            g.dma_start(out=qc_dram.ap()[1], in_=cf_c[:, :]).then_inc(s_qcw, 16)
            g.wait_ge(s_qcw, 32)
            for n2 in range(2):
                g.dma_start(out=q_pk[64 * n2:64 * (n2 + 1), :]
                            .rearrange("p (gg h) -> p gg h", gg=2),
                            in_=qc_dram.ap()[0]
                            .rearrange("(gg chsub) (n h) -> n chsub gg h",
                                       gg=2, n=2)[n2]).then_inc(s_qcr, 16)
                g.dma_start(out=c_pk[64 * n2:64 * (n2 + 1), :]
                            .rearrange("p (gg h) -> p gg h", gg=2),
                            in_=qc_dram.ap()[1]
                            .rearrange("(gg chsub) (n h) -> n chsub gg h",
                                       gg=2, n=2)[n2]).then_inc(s_qcr, 16)
            g.dma_start(out=wg_sb[:],
                        in_=wg_t.rearrange("(kc p) e -> p kc e", p=128)
                        ).then_inc(s_wsm, 16)
            g.wait_ge(s_snd1, 16 * NT)
            g.collective_compute(
                "AllToAll", OP.bypass,
                replica_groups=[list(range(CORES))],
                ins=[xn_snd_t.ap().opt()], outs=[xn_rcv_t.ap().opt()],
            ).then_inc(s_cc1, 1)
            g.wait_ge(s_cc1, 1)

            def spk_store(wp):
                g.wait_ge(s_sevc, 2 * (wp + 1))
                for gg in range(2):
                    g.dma_start(out=s_snd[wp // 4, 64 * gg:64 * (gg + 1),
                                          (wp % 4) * WIN:(wp % 4 + 1) * WIN],
                                in_=spk[0:64, wp % 2, gg, :]
                                ).then_inc(s_ssnd[wp % 2], 16)

            for w in range(NWIN):
                for gg in range(2):
                    k = 2 * w + gg
                    if k >= 2:
                        g.wait_ge(s_hrdy, k - 1)
                    xsrc = xn_rcv[w // 4, 64 * gg:64 * (gg + 1),
                                  (w % 4) * WIN:(w % 4 + 1) * WIN]
                    g.dma_start(out=xs_rep[0:64, k % 2, :],
                                in_=xsrc).then_inc(s_win[k % 2], 16)
                    g.dma_start(out=xs_rep[64:128, k % 2, :],
                                in_=xsrc).then_inc(s_win[k % 2], 16)
                if w >= 1:
                    spk_store(w - 1)
            spk_store(NWIN - 1)
            g.wait_ge(s_ssnd[0], 32 * (NWIN // 2))
            g.wait_ge(s_ssnd[1], 32 * (NWIN // 2))
            g.collective_compute(
                "AllToAll", OP.bypass,
                replica_groups=[list(range(CORES))],
                ins=[s_snd_t.ap().opt()], outs=[s_rcv_t.ap().opt()],
            ).then_inc(s_cc2, 1)
            g.wait_ge(s_cc2, 1)
            for tt in range(NT):
                if tt >= 2:
                    g.wait_ge(s_p6v, 16 * (tt - 1))
                g.dma_start(out=s_tile[:, tt % 2, :, :],
                            in_=s_rcv.rearrange("i ch t -> ch i t")
                            [:, :, tt * 128:(tt + 1) * 128]
                            ).then_inc(s_p6ls[tt % 2], 16)
                g.wait_ge(s_gst, 16 * 8 * (tt // TPC + 1))
                if tt >= 1:
                    g.wait_ge(s_p6v, 16 * tt)
                g.dma_start(out=gq_tile[:, :, :],
                            in_=g_stage.ap().rearrange("(dd p) t -> p dd t", p=128)
                            [:, :, tt * 128:(tt + 1) * 128]).then_inc(s_p6lg, 16)

        @blk.scalar
        def _(sc):
            sc.wait_ge(s_sm, 16 * 6)
            sc.activation(out=cf_p[:], in_=cf_dl[:], func=AF.Sigmoid)
            sc.activation(out=cf_a[:], in_=cf_al[:], func=AF.Sigmoid).then_inc(s_cf1, 1)
            sc.wait_ge(s_st, 1)
            sc.activation(out=st_sum[:, :], in_=st_var[:, :],
                          func=AF.Sqrt, bias=eps_c[:, 0:1]).then_inc(s_sq, 1)
            for q in range(TCN):
                sc.wait_ge(s_g5, q + 1)
                sc.activation(out=g_w1[:, TPC * q:TPC * q + TPC],
                              in_=g_d12[:, TPC * q:TPC * q + TPC],
                              func=AF.Sigmoid).then_inc(s_sg, 1)
            sc.activation(out=g_dt[:, 0:128], in_=ident[:, 0:128],
                          func=AF.Silu)
            sc.wait_ge(s_wdr, 16)
            for tc in range(min(2, TCN)):
                for e in range(E):
                    sc.dma_start(out=wb_sb[:, tc % 2, e, :],
                                 in_=w_dram[e:e + 1, tc * TCW:(tc + 1) * TCW]
                                 .broadcast_to([128, TCW])).then_inc(s_wb, 16)
            for tc in range(TCN):
                for dout in range(8):
                    k = tc * 8 + dout
                    sc.wait_ge(s_mm, 8 * (tc + 1))
                    if k >= 1:
                        sc.wait_ge(s_gst, 16 * k)
                    sc.activation(out=g_dt[:, :],
                                  in_=ps[:, dout // 2,
                                         (dout % 2) * TCW:(dout % 2 + 1) * TCW],
                                  func=AF.Silu).then_inc(s_si, 1)
                    sc.dma_start(out=g_stage.ap()
                                 .rearrange("(dd p) t -> p dd t", p=128)
                                 [:, dout, tc * TCW:(tc + 1) * TCW],
                                 in_=g_dt[:, :]).then_inc(s_gst, 16)
                if tc + 2 < TCN:
                    sc.wait_ge(s_xe, 8 * (tc + 1))
                    for e in range(E):
                        sc.dma_start(out=wb_sb[:, tc % 2, e, :],
                                     in_=w_dram[e:e + 1,
                                                (tc + 2) * TCW:(tc + 3) * TCW]
                                     .broadcast_to([128, TCW])).then_inc(s_wb, 16)

        @blk.tensor
        def _(t):
            t.wait_ge(s_id, 1)
            for idx in range(NT * KC):
                t.wait_ge(s_nrm, idx + 1)
                if idx >= 2:
                    t.wait_ge(s_ev, idx - 1)
                t.transpose(ps[:, 1 + idx % 2, 0:128],
                            t2_sb[:, idx % 2, :],
                            ident[:, :]).then_inc(s_tp, 1)
            t.wait_ge(s_wsm, 32)
            for q in range(TCN):
                t.wait_ge(s_xch, 16 * (q + 1))
                if q >= 1:
                    t.wait_ge(s_g2, q)
                for kc in range(KC):
                    t.matmul(ps[0:8, 4, 0:TCW], wg_sb[:, kc, :],
                             xn_ch[:, kc, :],
                             start=(kc == 0), stop=(kc == KC - 1)
                             ).then_inc(s_g1, 1 if kc == KC - 1 else 0)
                t.wait_ge(s_g2, q + 1)
                for j in range(TPC):
                    tt_i = TPC * q + j
                    if tt_i >= 2:
                        t.wait_ge(s_g4, tt_i - 1)
                    t.transpose(ps[0:128, 5 + tt_i % 2, 0:8],
                                lg_em[0:8, j * 128:(j + 1) * 128],
                                ident[0:8, 0:8]).then_inc(s_g3, 1)
            for q in range(TCN):
                t.wait_ge(s_g5b, q + 1)
                for j in range(TPC):
                    tt_i = TPC * q + j
                    if tt_i >= 2:
                        t.wait_ge(s_g7, tt_i - 1)
                    t.transpose(ps[0:8, 5 + tt_i % 2, 0:128],
                                w_tm[:, tt_i, :],
                                ident[:, :]).then_inc(s_g6, 1)
            def comb(jc):
                if jc == 0:
                    t.wait_ge(s_cst, 1)
                if jc >= 2:
                    t.wait_ge(s_sevc, jc - 1)
                t.wait_ge(s_scn, 8 * jc + 8)
                for h in range(8):
                    t.matmul(ps[0:64, 4 + jc % 2, 0:WIN],
                             cmat[:, (jc % 2) * 8 + h, :],
                             h_pk[:, (jc % 2) * 8 + h, :],
                             start=(h == 0), stop=(h == 7)
                             ).then_inc(s_cmb, 1)

            t.wait_ge(s_g7, NT)
            for tc in range(TCN):
                if tc >= 1:
                    t.wait_ge(s_si, 8 * tc)
                for e in range(E):
                    t.wait_ge(s_xe, tc * 8 + e + 1)
                    t.wait_ge(s_we[e], 16)
                    ins = None
                    for dout in range(8):
                        for kc in range(KC):
                            ins = t.matmul(
                                ps[:, dout // 2, (dout % 2) * TCW:(dout % 2 + 1) * TCW],
                                we_sb[:, e, kc, dout * 128:(dout + 1) * 128],
                                xn_eb[:, kc, :],
                                start=(e == 0 and kc == 0 and dout % 2 == 0),
                                stop=False)
                    ins.then_inc(s_mmE, 1)
                    jj = tc * 8 + e
                    if jj >= 1:
                        comb(jj - 1)
                for dout in range(8):
                    t.matmul(ps[:, dout // 2, (dout % 2) * TCW:(dout % 2 + 1) * TCW],
                             be_sb[0:8, dout * 128:(dout + 1) * 128],
                             w_emb[0:8, tc * TCW:(tc + 1) * TCW],
                             start=False, stop=(dout % 2 == 1)).then_inc(s_mm, 1)
            comb(TCN * E - 1)
            t.wait_ge(s_si, 8 * TCN)
            for idx in range(NT * KC):
                t.wait_ge(s_p6v, 2 * (idx + 1))
                if idx >= 2:
                    t.wait_ge(s_p6e, idx - 1)
                t.transpose(ps[:, 1 + idx % 2, 0:128],
                            t2_sb[:, idx % 2, :], ident[:, :]).then_inc(s_p6t, 1)

        @blk.vector
        def _(v):
            v.memset(eps_c[:, :], EPS)

            def spacer():
                v.tensor_copy(out=xs_f[:, 0:128], in_=ident[:, 0:128])

            for i in range(NT):
                v.wait_ge(s_x[i % 2], 16 * (i // 2 + 1))
                v.reduce_sum(st_sum[:, i:i + 1], x_tm[:, i % 2, :], axis=AX)
                v.tensor_tensor(out=scr[:], in0=x_tm[:, i % 2, :],
                                in1=x_tm[:, i % 2, :], op=OP.mult)
                v.reduce_sum(st_ex2[:, i:i + 1], scr[:], axis=AX
                             ).then_inc(s_ln, 1)
            spacer()
            v.tensor_scalar(out=st_nmu[:, :], in0=st_sum[:, :],
                            scalar1=-1.0 / D, scalar2=None, op0=OP.mult)
            spacer()
            v.tensor_scalar(out=st_ex2[:, :], in0=st_ex2[:, :],
                            scalar1=1.0 / D, scalar2=None, op0=OP.mult)
            spacer()
            v.tensor_tensor(out=st_var[:, :], in0=st_nmu[:, :], in1=st_nmu[:, :],
                            op=OP.mult)
            spacer()
            v.tensor_tensor(out=st_var[:, :], in0=st_ex2[:, :], in1=st_var[:, :],
                            op=OP.subtract)
            v.tensor_copy(out=xs_f[:, 0:128], in_=ident[:, 0:128]
                          ).then_inc(s_st, 1)
            v.wait_ge(s_sq, 1)
            spacer()
            v.reciprocal(out=st_r[:, :], in_=st_sum[:, :])
            v.tensor_scalar(out=st_ex2[:, :], in0=st_var[:, :],
                            scalar1=EPS, scalar2=None, op0=OP.add)
            spacer()
            v.tensor_tensor(out=st_aux[:, :], in0=st_r[:, :], in1=st_r[:, :],
                            op=OP.mult)
            spacer()
            v.tensor_tensor(out=st_aux[:, :], in0=st_aux[:, :], in1=st_ex2[:, :],
                            op=OP.mult)
            spacer()
            v.tensor_scalar(out=st_aux[:, :], in0=st_aux[:, :],
                            scalar1=-0.5, scalar2=1.5, op0=OP.mult, op1=OP.add)
            spacer()
            v.tensor_tensor(out=st_r[:, :], in0=st_r[:, :], in1=st_aux[:, :],
                            op=OP.mult)
            spacer()
            v.wait_ge(s_cf1, 1)
            v.wait_ge(s_sm, 16 * 6)
            spacer()
            v.tensor_tensor(out=cf_q[:], in0=cf_p[:], in1=cf_a[:], op=OP.mult)
            v.tensor_tensor(out=st_sum[:, :], in0=cf_bt[:], in1=cf_gm[:],
                            op=OP.mult)
            v.tensor_scalar(out=st_aux[:, :], in0=cf_p[:],
                            scalar1=0.25, scalar2=None, op0=OP.mult)
            v.tensor_scalar(out=cf_q[:], in0=cf_q[:], scalar1=-1.0, scalar2=1.0,
                            op0=OP.mult, op1=OP.add)
            v.tensor_tensor(out=cf_c[:], in0=st_sum[:, :], in1=st_aux[:, :],
                            op=OP.mult)
            spacer()
            v.tensor_copy(out=st_aux[:, 0:1], in_=cf_c[:, 0:1]).then_inc(s_cf2, 1)
            v.wait_ge(s_qcr, 64)
            v.wait_ge(s_msk, 1)
            spacer()
            v.tensor_tensor(
                out=cmat[:, :, :],
                in0=c_pk.ap().rearrange("p (tl a) -> p tl a", a=1)
                .broadcast_to([128, NS, 64]),
                in1=bmask.ap().rearrange("p (a c) -> p a c", a=1)
                .broadcast_to([128, NS, 64]),
                op=OP.mult)
            v.tensor_copy(out=xs_f[:, 0:128], in_=ident[:, 0:128]
                          ).then_inc(s_cst, 1)
            def p2_evac(j):
                if j % KC == 0 and j >= KC:
                    v.wait_ge(s_snd1, 16 * (j // KC))
                v.wait_ge(s_tp, j + 1)
                v.tensor_copy(out=xn_eb[:, j % KC, 0:128],
                              in_=ps[:, 1 + j % 2, 0:128]).then_inc(s_ev, 1)

            for i in range(NT):
                v.wait_ge(s_x[i % 2], 16 * ((NT + i) // 2 + 1))
                for kc in range(KC):
                    idx = i * KC + kc
                    if idx == 1:
                        spacer()
                    if idx >= 2:
                        v.wait_ge(s_tp, idx - 1)
                    ins = v.tensor_scalar(out=t2_sb[:, idx % 2, :],
                                          in0=x_tm[:, i % 2,
                                                   kc * 128:(kc + 1) * 128],
                                          scalar1=st_nmu[:, i:i + 1],
                                          scalar2=st_r[:, i:i + 1],
                                          op0=OP.add, op1=OP.mult)
                    if idx >= 1:
                        ins.then_inc(s_nrm, 1)
                    if idx >= 1:
                        p2_evac(idx - 1)
            v.tensor_copy(out=xs_f[:, 0:128], in_=ident[:, 0:128]
                          ).then_inc(s_nrm, 1)
            p2_evac(NT * KC - 1)
            for q in range(TCN):
                v.wait_ge(s_g1, q + 1)
                if q >= 1:
                    v.wait_ge(s_g3, TPC * q)
                v.tensor_scalar(out=lg_em[0:8, :], in0=ps[0:8, 4, 0:TCW],
                                scalar1=bg_sb[0:8, 0:1], scalar2=None,
                                op0=OP.add)
                v.tensor_copy(out=xs_f[:, 0:128], in_=ident[:, 0:128]
                              ).then_inc(s_g2, 1)
                for j in range(TPC):
                    tt_i = TPC * q + j
                    v.wait_ge(s_g3, tt_i + 1)
                    v.tensor_copy(out=lg_tm[:, tt_i, :],
                                  in_=ps[0:128, 5 + tt_i % 2, 0:8]
                                  ).then_inc(s_g4, 1)
                tls = list(range(TPC * q, TPC * q + TPC))
                for ti in tls:
                    v.reduce_max(g_m1[:, ti:ti + 1], lg_tm[:, ti, :], axis=AX)
                for ti in tls:
                    v.tensor_scalar(out=g_eq1[:, ti % TPC, :], in0=lg_tm[:, ti, :],
                                    scalar1=g_m1[:, ti:ti + 1], scalar2=None,
                                    op0=OP.is_equal)
                for ti in tls:
                    v.scalar_tensor_tensor(out=g_x1[:, ti % TPC, :],
                                           in0=g_eq1[:, ti % TPC, :],
                                           scalar=-1e30, in1=lg_tm[:, ti, :],
                                           op0=OP.mult, op1=OP.add)
                for ti in tls:
                    v.reduce_max(g_m2[:, ti:ti + 1], g_x1[:, ti % TPC, :], axis=AX)
                for ti in tls:
                    v.tensor_scalar(out=g_eq2[:, ti % TPC, :], in0=g_x1[:, ti % TPC, :],
                                    scalar1=g_m2[:, ti:ti + 1], scalar2=None,
                                    op0=OP.is_equal)
                v.tensor_tensor(out=g_d12[:, TPC * q:TPC * q + TPC],
                                in0=g_m1[:, TPC * q:TPC * q + TPC],
                                in1=g_m2[:, TPC * q:TPC * q + TPC],
                                op=OP.subtract).then_inc(s_g5, 1)
                v.wait_ge(s_sg, q + 1)
                spacer()
                v.tensor_scalar(out=g_w2[:, TPC * q:TPC * q + TPC],
                                in0=g_w1[:, TPC * q:TPC * q + TPC],
                                scalar1=-1.0, scalar2=1.0, op0=OP.mult, op1=OP.add)
                for ti in tls:
                    v.tensor_scalar(out=g_x1[:, ti % TPC, :], in0=g_eq1[:, ti % TPC, :],
                                    scalar1=g_w1[:, ti:ti + 1], scalar2=None,
                                    op0=OP.mult)
                for ti in tls:
                    v.scalar_tensor_tensor(out=w_tm[:, ti, :],
                                           in0=g_eq2[:, ti % TPC, :],
                                           scalar=g_w2[:, ti:ti + 1],
                                           in1=g_x1[:, ti % TPC, :],
                                           op0=OP.mult, op1=OP.add)
                spacer()
                v.tensor_copy(out=xs_f[:, 0:128], in_=ident[:, 0:128]
                              ).then_inc(s_g5b, 1)
            for tt_i in range(NT):
                v.wait_ge(s_g6, tt_i + 1)
                v.tensor_copy(out=w_emb[0:8, tt_i * 128:(tt_i + 1) * 128],
                              in_=ps[0:8, 5 + tt_i % 2, 0:128]
                              ).then_inc(s_g7, 1)
            scan_work = []

            def emit_scans(budget):
                n = 0
                while scan_work and n < budget:
                    scan_work.pop(0)()
                    n += 1

            for w in range(NWIN):
                for gg in range(2):
                    def grp(w=w, gg=gg):
                        k = 2 * w + gg
                        v.wait_ge(s_win[k % 2], 32 * (k // 2 + 1))
                        if k >= 2:
                            v.wait_ge(s_cmb, 8 * k - 8)
                            km = k - 2
                            wm, ggm = divmod(km, 2)
                            if wm >= 2:
                                v.wait_ge(s_ssnd[wm % 2], 32 * (wm // 2))
                            v.tensor_copy(out=spk[0:64, wm % 2, ggm, :],
                                          in_=ps[0:64, 4 + km % 2, 0:WIN])
                        for h in range(8):
                            tl = gg * 8 + h
                            q_b = q_pk[:, tl:tl + 1].broadcast_to([128, WIN])
                            if w % WPB == 0:
                                ins = v.tensor_tensor_scan(
                                    out=h_pk[:, (k % 2) * 8 + h, :],
                                    data0=q_b, data1=xs_rep[:, k % 2, :],
                                    initial=0.0, op0=OP.mult, op1=OP.add)
                            else:
                                ins = v.tensor_tensor_scan(
                                    out=h_pk[:, (k % 2) * 8 + h, :],
                                    data0=q_b, data1=xs_rep[:, k % 2, :],
                                    initial=car_pk[:, tl:tl + 1],
                                    op0=OP.mult, op1=OP.add)
                            if h == 0 and k >= 2:
                                ins.then_inc(s_sevc, 1)
                            elif h >= 1:
                                ins.then_inc(s_scn, 1)
                        for h in range(8):
                            tl = gg * 8 + h
                            ins = v.tensor_copy(
                                out=car_pk[:, tl:tl + 1],
                                in_=h_pk[:, (k % 2) * 8 + h, WIN - 1:WIN])
                            if h == 0:
                                ins.then_inc(s_hrdy, 1)
                            if h == 7:
                                ins.then_inc(s_scn, 1)
                    scan_work.append(grp)

            per_slot = 1
            for tc in range(TCN):
                for e in range(E):
                    gi = tc * 8 + e
                    if gi >= 1:
                        v.wait_ge(s_mmE, gi)
                    v.wait_ge(s_wb, 256 if tc < 2 else 256 + 128 * (tc - 1))
                    v.wait_ge(s_xch, 16 * (TCN + tc + 1))
                    v.tensor_tensor(
                        out=xn_eb[:, :, :],
                        in0=xn_ch[:, :, :],
                        in1=wb_sb[:, tc % 2, e, :]
                        .rearrange("p (a t) -> p a t", a=1)
                        .broadcast_to([128, KC, TCW]),
                        op=OP.mult)
                    v.tensor_copy(out=t1_sb[:, :], in_=ident[:, :])
                    v.tensor_copy(out=t1_sb[:, :], in_=ident[:, :]
                                  ).then_inc(s_xe, 1)
                    emit_scans(per_slot)
            emit_scans(10 ** 9)
            for km in (2 * NWIN - 2, 2 * NWIN - 1):
                wm, ggm = divmod(km, 2)
                v.wait_ge(s_ssnd[wm % 2], 32 * (wm // 2))
                v.wait_ge(s_cmb, 8 * km + 8)
                v.tensor_copy(out=spk[0:64, wm % 2, ggm, :],
                              in_=ps[0:64, 4 + km % 2, 0:WIN])
                v.tensor_copy(out=xs_f[:, 0:128], in_=ident[:, 0:128]
                              ).then_inc(s_sevc, 1)
            def p6_evac(ix, also_p6v):
                v.wait_ge(s_p6t, ix + 1)
                if ix >= 3:
                    v.wait_ge(s_out[ix % 3], 16 * (ix // 3))
                v.tensor_tensor(out=o_stage[:, ix % 3, :],
                                in0=ps[:, 1 + ix % 2, 0:128],
                                in1=x_tm[:, (ix // KC) % 2,
                                         (ix % KC) * 128:(ix % KC) * 128 + 128],
                                op=OP.add).then_inc(s_p6e, 1)
                if also_p6v:
                    v.tensor_copy(out=xs_f[:, 0:8], in_=ident[:, 0:8]
                                  ).then_inc(s_p6v, 2)

            for idx in range(NT * KC):
                tt, i = divmod(idx, KC)
                if i == 0 and idx >= 1:
                    p6_evac(idx - 1, False)
                if i == 0:
                    v.wait_ge(s_p6ls[tt % 2], 16 * (tt // 2 + 1))
                    v.wait_ge(s_p6lg, 16 * (tt + 1))
                    v.wait_ge(s_xch, 16 * (2 * TCN + tt + 1))
                    v.wait_ge(s_x[tt % 2], 16 * ((2 * NT + tt) // 2 + 1))
                v.scalar_tensor_tensor(out=t1_sb[:, :],
                                       in0=xn_ch[:, i, 0:128],
                                       scalar=om_sb[:, i:i + 1],
                                       in1=s_tile[:, tt % 2, i, :],
                                       op0=OP.mult, op1=OP.add)
                v.tensor_tensor(out=t2_sb[:, idx % 2, :], in0=t1_sb[:, :],
                                in1=gq_tile[:, i, :], op=OP.mult)
                if i == 0:
                    v.tensor_copy(out=xs_f[:, 0:128], in_=ident[:, 0:128]
                                  ).then_inc(s_p6v, 2)
                else:
                    p6_evac(idx - 1, True)
            p6_evac(NT * KC - 1, False)
    return nc


_NC_CACHE = {}


def _get_nc():
    if "nc" not in _NC_CACHE:
        _NC_CACHE["nc"] = build_nc()
    return _NC_CACHE["nc"]


def kernel(L0_attn_x, L0_attn_delta, L0_attn_alpha, L0_attn_beta, L0_attn_gamma,
           L0_attn_omega, L0_ffn_We, L0_ffn_be, L0_ffn_Wg, L0_ffn_bg):
    x = np.ascontiguousarray(np.asarray(L0_attn_x, dtype=np.float32)
                             .reshape(16384, 1024))
    we_t = np.ascontiguousarray(np.asarray(L0_ffn_We, np.float32).transpose(0, 2, 1))
    wg_t = np.ascontiguousarray(np.asarray(L0_ffn_Wg, np.float32).T)
    be = np.ascontiguousarray(np.asarray(L0_ffn_be, np.float32))
    bg = np.ascontiguousarray(np.asarray(L0_ffn_bg, np.float32).reshape(8, 1))
    om = np.ascontiguousarray(np.asarray(L0_attn_omega, np.float32)
                              .reshape(8, 128).T)
    dl = np.asarray(L0_attn_delta, np.float32)[:, :, 0]
    al = np.asarray(L0_attn_alpha, np.float32)[:, :, 0]
    bt = np.asarray(L0_attn_beta, np.float32)[:, :, 0]
    gm = np.asarray(L0_attn_gamma, np.float32)

    in_maps = []
    for j in range(CORES):
        sl = slice(j * TOK, (j + 1) * TOK)
        ch = slice(j * 128, (j + 1) * 128)
        in_maps.append({
            "x_slab": np.ascontiguousarray(x[sl]),
            "we_t": we_t, "wg_t": wg_t, "be_in": be, "bg_in": bg, "om_in": om,
            "dl_in": np.ascontiguousarray(dl[ch]),
            "al_in": np.ascontiguousarray(al[ch]),
            "bt_in": np.ascontiguousarray(bt[ch]),
            "gm_in": np.ascontiguousarray(gm[ch]),
        })
    nc = _get_nc()
    res = run_bass_kernel_spmd(nc, in_maps, core_ids=list(range(CORES)))
    out = np.concatenate([r["out_slab"] for r in res.results], axis=0)
    return out.reshape(2, 8192, 1024)
